# version 05
# baseline (speedup 1.0000x reference)
import numpy as np
import ml_dtypes
import concourse.bass as bass
import concourse.mybir as mybir
from concourse.bass_utils import run_bass_kernel_spmd

F32 = mybir.dt.float32
BF16 = mybir.dt.bfloat16
I32 = mybir.dt.int32
AF = mybir.ActivationFunctionType
ALU = mybir.AluOpType
AX = mybir.AxisListType
ET = mybir.EngineType

B = 2
T = 16384
C = 1024
FF = 4096
NCORES = 8
HALF = T // 2
ST = 512
NST1 = HALF // ST
NSLOT = 16
QROWS = NSLOT * 256
RECIP_N = 1.0 / float(C)
EPS = 1e-5
MAGIC = 192.0


class Ctr:
    def __init__(self):
        self.v = 0

    def inc(self, n):
        self.v += n
        return self.v


class _SemProxy:
    def __init__(self, hmap, name):
        self.hmap = hmap
        self.name = name

    def __getattr__(self, a):
        return getattr(self.hmap[self.name], a)


def build_program(debug_taps=False, detect_races=True, stage="all"):
    nc = bass.Bass("TRN2", target_bir_lowering=False,
                   detect_race_conditions=detect_races)

    x_prod = nc.dram_tensor("x_prod", [HALF, C], F32, kind="ExternalInput").ap()
    w1t_in = nc.dram_tensor("w1t_in", [C, FF], F32, kind="ExternalInput").ap()
    w2t_in = nc.dram_tensor("w2t_in", [FF, C], F32, kind="ExternalInput").ap()
    qkvt_in = nc.dram_tensor("qkvt_in", [C, 3 * C], F32, kind="ExternalInput").ap()
    outt_in = nc.dram_tensor("outt_in", [C, C], F32, kind="ExternalInput").ap()
    b1_in = nc.dram_tensor("b1_in", [128, FF // 128], F32, kind="ExternalInput").ap()
    b2_in = nc.dram_tensor("b2_in", [128, C // 128], F32, kind="ExternalInput").ap()
    outb_in = nc.dram_tensor("outb_in", [128, C // 128], F32, kind="ExternalInput").ap()
    ident_in = nc.dram_tensor("ident_in", [128, 128], BF16, kind="ExternalInput").ap()
    mask_in = nc.dram_tensor("mask_in", [128, 8, 256], BF16, kind="ExternalInput").ap()
    meta_in = nc.dram_tensor("meta_in", [1, 8], I32, kind="ExternalInput").ap()
    y_out = nc.dram_tensor("y_out", [QROWS, C], F32, kind="ExternalOutput").ap()
    if debug_taps:
        d_x1 = nc.dram_tensor("d_x1", [T, C], F32, kind="ExternalOutput").ap()
        d_q = nc.dram_tensor("d_q", [C, T], F32, kind="ExternalOutput").ap()
        d_k = nc.dram_tensor("d_k", [C, T], F32, kind="ExternalOutput").ap()
        d_v = nc.dram_tensor("d_v", [T, C], F32, kind="ExternalOutput").ap()
        d_x2 = nc.dram_tensor("d_x2", [QROWS, C], F32, kind="ExternalOutput").ap()
        d_w1t = nc.dram_tensor("d_w1t", [C, FF], F32, kind="ExternalOutput").ap()
        d_scal = nc.dram_tensor("d_scal", [128, 4], F32, kind="ExternalOutput").ap()

    sh_x1 = nc.dram_tensor("sh_x1", [T, C], F32, kind="Internal", addr_space="Shared").ap()
    sh_kT = nc.dram_tensor("sh_kT", [C, T], BF16, kind="Internal", addr_space="Shared").ap()
    sh_qT = nc.dram_tensor("sh_qT", [C, T], BF16, kind="Internal", addr_space="Shared").ap()
    sh_v = nc.dram_tensor("sh_v", [T, C], BF16, kind="Internal", addr_space="Shared").ap()
    sh_flag = nc.dram_tensor("sh_flag", [2, 16], I32, kind="Internal", addr_space="Shared").ap()
    w1t_st = nc.dram_tensor("w1t_st", [C, FF], BF16, kind="Internal").ap()
    w2t_st = nc.dram_tensor("w2t_st", [FF, C], BF16, kind="Internal").ap()
    x2_dram = nc.dram_tensor("x2_dram", [QROWS, C], F32, kind="Internal").ap()

    ops = {k: [] for k in ("sync", "scalar", "vector", "tensor", "gpsimd")}

    def em(eng, fn):
        if eng == "vector":
            ops[eng].append((fn, ctr["vp"].v))
        elif eng == "scalar":
            ops[eng].append((fn, ctr["ap"].v))
        else:
            ops[eng].append((fn, None))

    ctr = {k: Ctr() for k in ("sq", "gq", "tp", "ap", "vp", "bar")}
    sq_h = nc.ctx.enter_context(nc.semaphore("sq"))
    gq_h = nc.ctx.enter_context(nc.semaphore("gq"))
    pq_h = nc.ctx.enter_context(nc.semaphore("pq"))
    tp_h = nc.ctx.enter_context(nc.semaphore("tp"))
    ap_h = nc.ctx.enter_context(nc.semaphore("ap"))
    vp_h = nc.ctx.enter_context(nc.semaphore("vp"))
    bar_h = nc.ctx.enter_context(nc.semaphore("bar"))
    sw_h = nc.ctx.enter_context(nc.semaphore("sw"))
    qst_h = [nc.ctx.enter_context(nc.semaphore("qst0")),
             nc.ctx.enter_context(nc.semaphore("qst1"))]
    skv_h = [nc.ctx.enter_context(nc.semaphore("skv0")),
             nc.ctx.enter_context(nc.semaphore("skv1"))]
    gqq_h = [nc.ctx.enter_context(nc.semaphore("gqq0")),
             nc.ctx.enter_context(nc.semaphore("gqq1"))]
    ctr["sw"] = Ctr()
    ctr["qst0"] = Ctr()
    ctr["qst1"] = Ctr()
    ctr["skv0"] = Ctr()
    ctr["skv1"] = Ctr()
    ctr["gqq0"] = Ctr()
    ctr["gqq1"] = Ctr()

    r_role_g = nc.alloc_register(ET.Pool, "r_role_g")
    r_tmp_g = nc.alloc_register(ET.Pool, "r_tmp_g")
    r_exp_g = nc.alloc_register(ET.Pool, "r_exp_g")
    r_val_g = nc.alloc_register(ET.Pool, "r_val_g")
    r_diff_g = nc.alloc_register(ET.Pool, "r_diff_g")
    r_sc_g = nc.alloc_register(ET.Pool, "r_sc_g")
    r_q_g = nc.alloc_register(ET.Pool, "r_q_g")
    r_off_g = nc.alloc_register(ET.Pool, "r_off_g")
    r_role_s = nc.alloc_register(ET.SP, "r_role_s")

    ctx = nc.ctx

    ident_sb = ctx.enter_context(nc.sbuf_tensor("ident_sb", [128, 128], BF16))
    b1_sb = ctx.enter_context(nc.sbuf_tensor("b1_sb", [128, FF // 128], F32))
    b2h_sb = ctx.enter_context(nc.sbuf_tensor("b2h_sb", [128, C // 128], F32))
    outb_sb = ctx.enter_context(nc.sbuf_tensor("outb_sb", [128, C // 128], F32))
    meta_sb = ctx.enter_context(nc.sbuf_tensor("meta_sb", [1, 8], I32))
    flag_sb = ctx.enter_context(nc.sbuf_tensor("flag_sb", [1, 1], I32))
    ones_sb = ctx.enter_context(nc.sbuf_tensor("ones_sb", [128, 1], BF16))
    onesf_sb = ctx.enter_context(nc.sbuf_tensor("onesf_sb", [128, 1], F32))
    onesrow_sb = ctx.enter_context(nc.sbuf_tensor("onesrow_sb", [1, 128], F32))
    mask_sb = ctx.enter_context(nc.sbuf_tensor("mask_sb", [128, 8, 256], BF16))
    scal_sb = ctx.enter_context(nc.sbuf_tensor("scal_sb", [128, 4], F32))
    eps_sb = ctx.enter_context(nc.sbuf_tensor("eps_sb", [128, 1], F32))

    def _emit_taps():
        if not debug_taps:
            return
        c_pre_taps = ctr["sq"].v

        c_pre_q0s = ctr["qst0"].v
        c_pre_q1s = ctr["qst1"].v

        def s_taps(s):
            s.wait_ge(sq_h, c_pre_taps)
            s.wait_ge(qst_h[0], c_pre_q0s)
            s.wait_ge(qst_h[1], c_pre_q1s)
            s.dma_start(out=d_x1, in_=sh_x1).then_inc(sq_h, 16)
            s.dma_start(out=d_x2, in_=x2_dram).then_inc(sq_h, 16)
        em("sync", s_taps)
        ctr["sq"].inc(32)

        c_tap_end = ctr["gq"].v + 80

        c_pre_q0 = ctr["qst0"].v
        c_pre_q1 = ctr["qst1"].v

        def g_taps(g, c_pre_taps=c_pre_taps, c_tap_end=c_tap_end,
                   c_pre_q0=c_pre_q0, c_pre_q1=c_pre_q1):
            g.wait_ge(sq_h, c_pre_taps)
            g.wait_ge(qst_h[0], c_pre_q0)
            g.wait_ge(qst_h[1], c_pre_q1)
            g.dma_start(out=d_q, in_=sh_qT).then_inc(gq_h, 16)
            g.dma_start(out=d_k, in_=sh_kT).then_inc(gq_h, 16)
            g.dma_start(out=d_v, in_=sh_v).then_inc(gq_h, 16)
            g.dma_start(out=d_w1t, in_=w1t_st).then_inc(gq_h, 16)
            g.dma_start(out=d_scal, in_=scal_sb[:, :]).then_inc(gq_h, 16)
            g.wait_ge(gq_h, c_tap_end)
        em("gpsimd", g_taps)
        ctr["gq"].inc(80)

    def _emit_blocks():
      with nc.Block() as block:
          @block.sync
          def _(s):
              for fn, _pre in ops["sync"]:
                  fn(s)

          @block.gpsimd
          def _(g):
              for fn, _pre in ops["gpsimd"]:
                  fn(g)

          @block.tensor
          def _(t):
              for fn, _pre in ops["tensor"]:
                  fn(t)

          @block.scalar
          def _(sc):
              for fn, pre in ops["scalar"]:
                  if pre:
                      sc.wait_ge(ap_h, pre)
                  fn(sc)

          @block.vector
          def _(v):
              for fn, pre in ops["vector"]:
                  if pre:
                      v.wait_ge(vp_h, pre)
                  fn(v)


    def _finalize():
        _emit_taps()
        c_all_sq = ctr["sq"].v
        c_all_q0 = ctr["qst0"].v
        c_all_q1 = ctr["qst1"].v

        def s_fin(s):
            s.wait_ge(sq_h, c_all_sq)
            s.wait_ge(qst_h[0], c_all_q0)
            s.wait_ge(qst_h[1], c_all_q1)
        em("sync", s_fin)
        _emit_blocks()
        return nc
    

    def s_loads(s):
        s.dma_start(out=ident_sb[:, :], in_=ident_in).then_inc(sq_h, 16)
        s.dma_start(out=b1_sb[:, :], in_=b1_in).then_inc(sq_h, 16)
        s.dma_start(out=b2h_sb[:, :], in_=b2_in).then_inc(sq_h, 16)
        s.dma_start(out=outb_sb[:, :], in_=outb_in).then_inc(sq_h, 16)
        s.dma_start(out=meta_sb[:, :], in_=meta_in).then_inc(sq_h, 16)
        s.dma_start(out=mask_sb[:, :, :], in_=mask_in).then_inc(sq_h, 16)
    em("sync", s_loads)
    c_consts = ctr["sq"].inc(6 * 16)

    def v_consts(v):
        v.wait_ge(sq_h, c_consts)
        v.memset(ones_sb[:, :], 1.0)
        v.memset(onesf_sb[:, :], 1.0)
        v.memset(onesrow_sb[:, :], 1.0)
        v.memset(eps_sb[:, :], EPS)
        v.tensor_scalar_mul(b2h_sb[:, :], b2h_sb[:, :], 0.5).then_inc(vp_h, 1)
    em("vector", v_consts)
    c_vconst = ctr["vp"].inc(1)

    def g_meta(g):
        g.wait_ge(sq_h, c_consts)
        g.reg_load(r_role_g, meta_sb[:1, 0:1])
        g.reg_load(r_q_g, meta_sb[:1, 3:4])
    em("gpsimd", g_meta)

    def s_meta(s):
        s.wait_ge(sq_h, c_consts)
        s.reg_load(r_role_s, meta_sb[:1, 0:1])
    em("sync", s_meta)

    if stage == "p1b_only":
        def s_seed(s):
            sh_r = sh_x1.rearrange("(n p) c -> p n c", p=128)
            xr = x_prod.rearrange("(n p) c -> p n c", p=128)
            with s.If_eq(r_role_s, 0):
                s.dma_start(out=sh_r[:, 0:HALF // 128, :], in_=xr).then_inc(sq_h, 16)
            with s.Else():
                s.dma_start(out=sh_r[:, HALF // 128:T // 128, :], in_=xr).then_inc(sq_h, 16)
        em("sync", s_seed)
        c_p1a_done = ctr["sq"].inc(16)
    else:
        w1_ctx = nc.sbuf_tensor("w1t_sb", [128, 8, FF], BF16)
        w2_ctx = nc.sbuf_tensor("w2t_sb", [128, 32, C], BF16)
        w1t_sb = w1_ctx.__enter__()
        w2t_sb = w2_ctx.__enter__()

    def ternarize(win_ap, n_rows, n_free, dst_view, dst_st, scale_col, extra_scale,
                  scratch_w, scratch_t, part_sb, par2_sb, ps_small):
        nt = n_rows // 128
        total = float(n_rows * n_free)
        for i in range(nt):
            def s_load(s, i=i, gate=(i == 0)):
                if gate:
                    s.wait_ge(vp_h, c_vconst)
                s.dma_start(out=scratch_w[:, 0:n_free],
                            in_=win_ap[i * 128:(i + 1) * 128, :]).then_inc(sq_h, 16)
            em("sync", s_load)
            c_l = ctr["sq"].inc(16)

            def v_abs(v, i=i, c_l=c_l):
                v.wait_ge(sq_h, c_l)
                v.reduce_sum(part_sb[:, i:i + 1], scratch_w[:, 0:n_free], AX.X,
                             apply_absolute_value=True).then_inc(vp_h, 1)
            em("vector", v_abs)
            c_abs = ctr["vp"].inc(1)

            def s_wait(s, c_abs=c_abs):
                s.wait_ge(vp_h, c_abs)
            em("sync", s_wait)

        def v_part(v):
            v.reduce_sum(par2_sb[:, 0:1], part_sb[:, 0:nt], AX.X).then_inc(vp_h, 1)
        em("vector", v_part)
        c_p = ctr["vp"].inc(1)

        def t_psum(t, c_p=c_p):
            t.wait_ge(vp_h, c_p)
            t.matmul(ps_small[0:1, 0:1], lhsT=par2_sb[:, 0:1],
                     rhs=onesf_sb[:, 0:1], start=True, stop=True).then_inc(tp_h, 1)
        em("tensor", t_psum)
        c_mm = ctr["tp"].inc(1)

        base_sc = ctr["vp"].v

        def v_scale(v, c_mm=c_mm, base=base_sc):
            v.wait_ge(tp_h, c_mm)
            v.tensor_scalar(par2_sb[:1, 1:2], ps_small[0:1, 0:1],
                            1.0 / total, 1e-8, ALU.mult, ALU.max).then_inc(vp_h, 1)
            v.wait_ge(vp_h, base + 1)
            v.reciprocal(part_sb[:1, 35:36], par2_sb[:1, 1:2]).then_inc(vp_h, 1)
        em("vector", v_scale)
        c_r = ctr["vp"].inc(2)

        def t_bcast(t, c_r=c_r):
            t.wait_ge(vp_h, c_r)
            t.matmul(ps_small[:, 1:2], lhsT=onesrow_sb[:1, :],
                     rhs=part_sb[:1, 35:36], start=True, stop=True).then_inc(tp_h, 1)
        em("tensor", t_bcast)
        c_bc = ctr["tp"].inc(1)

        def v_bc(v, c_bc=c_bc):
            v.wait_ge(tp_h, c_bc)
            v.tensor_copy(scal_sb[:, scale_col:scale_col + 1], ps_small[:, 1:2]).then_inc(vp_h, 1)
        em("vector", v_bc)
        c_sc = ctr["vp"].inc(1)

        for i in range(nt):
            def s_load2(s, i=i):
                s.dma_start(out=scratch_w[:, 0:n_free],
                            in_=win_ap[i * 128:(i + 1) * 128, :]).then_inc(sq_h, 16)
            em("sync", s_load2)
            c_l2 = ctr["sq"].inc(16)

            base_q = ctr["vp"].v

            def v_quant(v, i=i, c_l2=c_l2, c_sc=c_sc, base=base_q):
                v.wait_ge(sq_h, c_l2)
                if i == 0:
                    v.wait_ge(tp_h, c_bc)
                v.tensor_scalar(scratch_t[:, 0:n_free], scratch_w[:, 0:n_free],
                                scal_sb[:, scale_col:scale_col + 1], MAGIC,
                                ALU.mult, ALU.add).then_inc(vp_h, 1)
                v.wait_ge(vp_h, base + 1)
                v.tensor_scalar(scratch_t[:, 0:n_free], scratch_t[:, 0:n_free],
                                MAGIC + 1.0, MAGIC - 1.0, ALU.min, ALU.max).then_inc(vp_h, 1)
                v.wait_ge(vp_h, base + 2)
                v.tensor_scalar(dst_view(i), scratch_t[:, 0:n_free],
                                extra_scale, -MAGIC * extra_scale,
                                ALU.mult, ALU.add).then_inc(vp_h, 1)
            em("vector", v_quant)
            c_q = ctr["vp"].inc(3)

            def s_stash(s, i=i, c_q=c_q):
                s.wait_ge(vp_h, c_q)
                s.dma_start(out=dst_st[i * 128:(i + 1) * 128, :],
                            in_=dst_view(i)).then_inc(sq_h, 16)
            em("sync", s_stash)
            ctr["sq"].inc(16)

    if stage != "p1b_only":
        with nc.sbuf_tensor("p0_w", [128, FF], F32) as p0_w, \
             nc.sbuf_tensor("p0_t", [128, FF], BF16) as p0_t, \
             nc.sbuf_tensor("p0_part", [128, 36], F32) as p0_part, \
             nc.sbuf_tensor("p0_par2", [128, 4], F32) as p0_par2, \
             nc.psum_tensor("p0_ps", [128, 512], F32) as p0_ps:
            ternarize(w1t_in, C, FF, lambda i: w1t_sb[:, i, :], w1t_st, 0, 1.0,
                      p0_w, p0_t, p0_part, p0_par2, p0_ps)
            ternarize(w2t_in, FF, C, lambda i: w2t_sb[:, i, :], w2t_st, 1, 0.5,
                      p0_w, p0_t, p0_part, p0_par2, p0_ps)

    def emit_ln(x_view, stA, stB, stC, ln_view, c_xready, prev_tr_tp, xr_sem=None):
        base_st = ctr["vp"].v

        def v_stats(v, c_xready=c_xready, ptr=prev_tr_tp, base=base_st,
                    xs=xr_sem):
            v.wait_ge(xs if xs is not None else sq_h, c_xready)
            if ptr is not None:
                v.wait_ge(tp_h, ptr)
            v.bn_stats(stA[:, 0:6], x_view[:, 0:512])
            v.bn_stats(stA[:, 6:12], x_view[:, 512:1024]).then_inc(vp_h, 1)
            v.wait_ge(vp_h, base + 1)
            v.bn_aggr(stB[:, 0:2], stA[:, 0:12]).then_inc(vp_h, 1)
        em("vector", v_stats)
        c_var = ctr["vp"].inc(2)

        def a_sqrt(sc, c_var=c_var):
            sc.wait_ge(vp_h, c_var)
            sc.activation(stC[:, 1:2], stB[:, 1:2], AF.Sqrt, bias=eps_sb[:, :]).then_inc(ap_h, 1)
        em("scalar", a_sqrt)
        c_std = ctr["ap"].inc(1)

        base_fin = ctr["vp"].v

        def v_fin(v, c_std=c_std, base=base_fin):
            v.wait_ge(ap_h, c_std)
            v.reciprocal(stB[:, 2:3], stC[:, 1:2]).then_inc(vp_h, 1)
            v.wait_ge(vp_h, base + 1)
            v.scalar_tensor_tensor(stC[:, 2:3], stB[:, 0:1], -1.0, stB[:, 2:3],
                                   ALU.mult, ALU.mult).then_inc(vp_h, 1)
            v.wait_ge(vp_h, base + 2)
            v.tensor_scalar(ln_view, x_view, stB[:, 2:3], stC[:, 2:3],
                            ALU.mult, ALU.add).then_inc(vp_h, 1)
        em("vector", v_fin)
        c_ln = ctr["vp"].inc(3)
        return c_ln, c_var

    def emit_transposes(src_fn, nchunks, ps_bank, dst_fn, c_src_v, extra_t_waits=()):
        def t_tr(t, waits=tuple(extra_t_waits), c_src_v=c_src_v):
            for sem, val in waits:
                t.wait_ge(sem, val)
            t.wait_ge(vp_h, c_src_v)
            for cc in range(nchunks):
                mm = t.transpose(ps_bank[:, cc * 128:(cc + 1) * 128],
                                 src_fn(cc), ident_sb[:, :])
            mm.then_inc(tp_h, 1)
        em("tensor", t_tr)
        c_tr = ctr["tp"].inc(1)

        def a_evac(sc, c_tr=c_tr):
            sc.wait_ge(tp_h, c_tr)
            for cc in range(nchunks):
                ins = sc.copy(dst_fn(cc), ps_bank[:, cc * 128:(cc + 1) * 128])
            ins.then_inc(ap_h, 1)
        em("scalar", a_evac)
        return ctr["ap"].inc(1), c_tr

    def ffn_phase(x_src, nst, out_shared, w1s, w2s, first_load_vp=None,
                  first_tr_sq=None, first_load_sq=None, tag=""):
        with nc.sbuf_tensor("f_x" + tag, [128, 4, C], F32) as x_sb, \
             nc.sbuf_tensor("f_stA" + tag, [128, 12], F32) as stA, \
             nc.sbuf_tensor("f_stB" + tag, [128, 4], F32) as stB, \
             nc.sbuf_tensor("f_stC" + tag, [128, 4], F32) as stC, \
             nc.sbuf_tensor("f_ln" + tag, [128, 2, C], BF16) as ln_sb, \
             nc.sbuf_tensor("f_lnT" + tag, [128, 8, ST], BF16) as lnT_sb, \
             nc.sbuf_tensor("f_h" + tag, [128, 32, ST], BF16) as h_sb, \
             nc.sbuf_tensor("f_ffn" + tag, [128, 8, ST], BF16) as ffn_sb, \
             nc.psum_tensor("f_ptr01" + tag, [128, 2048], BF16) as ps_tr01, \
             nc.psum_tensor("f_ph01" + tag, [128, 1024], F32) as ps_h01, \
             nc.psum_tensor("f_pd4" + tag, [128, 2048], F32) as ps_d4:
            ps_tr = [ps_tr01[:, 0:1024], ps_tr01[:, 1024:2048]]
            ps_h = [ps_h01[:, 0:512], ps_h01[:, 512:1024]]
            ps_d = [ps_d4[:, i * 512:(i + 1) * 512] for i in range(4)]
            prev_store = None
            prev_res_last = None
            prev_evB = None
            prev_h_free = None
            tr_marks = []
            stats_marks = []
            for st_i in range(nst):
                base = st_i * ST
                def s_x(s, base=base, pv=prev_store,
                        flw=first_load_vp if st_i == 0 else None,
                        fls=first_load_sq if st_i == 0 else None):
                    if pv is not None:
                        s.wait_ge(sq_h, pv)
                    if flw is not None:
                        s.wait_ge(vp_h, flw)
                    if fls is not None:
                        s.wait_ge(sq_h, fls)
                    s.dma_start(out=x_sb[:, :, :],
                                in_=x_src.rearrange("(n p) c -> p n c", p=128)
                                [:, base // 128:base // 128 + 4, :]
                                ).then_inc(sq_h, 16)
                em("sync", s_x)
                c_x = ctr["sq"].inc(16)
                c_xt = [c_x] * 4
                lnT_ready = None
                for tt in range(4):
                    ptr = tr_marks[-2] if len(tr_marks) >= 2 else None
                    psv = stats_marks[-1] if stats_marks else None
                    c_ln, c_var = emit_ln(x_sb[:, tt, :], stA, stB, stC,
                                          ln_sb[:, tt % 2, :], c_xt[tt], ptr)
                    stats_marks.append(c_var)
                    extra = []
                    if tt == 0 and first_tr_sq is not None and st_i == 0:
                        extra.append((sw_h, first_tr_sq))
                    if tt == 0 and prev_h_free is not None:
                        extra.append((tp_h, prev_h_free))
                    if tt == 0 and prev_res_last is not None:
                        extra.append((vp_h, prev_res_last))
                    cev, c_tr = emit_transposes(
                        lambda cc, tt=tt: ln_sb[:, tt % 2, cc * 128:(cc + 1) * 128],
                        8, ps_tr[tt % 2],
                        lambda cc, tt=tt: lnT_sb[:, cc, tt * 128:(tt + 1) * 128],
                        c_ln, extra)
                    tr_marks.append(c_tr)
                    lnT_ready = cev
                gelu_c = []
                for ff in range(33):
                    if ff < 32:
                        f = ff
                        def t_w1(t, f=f,
                                 lnr=lnT_ready if f == 0 else None,
                                 pevB=prev_evB if f == 0 else None,
                                 gw=gelu_c[-2] if len(gelu_c) >= 2 else None):
                            if lnr is not None:
                                t.wait_ge(ap_h, lnr)
                            if pevB is not None:
                                t.wait_ge(vp_h, pevB)
                            if gw is not None:
                                t.wait_ge(ap_h, gw)
                            for cc in range(8):
                                mm = t.matmul(ps_h[f % 2],
                                              lhsT=w1s[:, cc, f * 128:(f + 1) * 128],
                                              rhs=lnT_sb[:, cc, :],
                                              start=(cc == 0), stop=(cc == 7))
                            mm.then_inc(tp_h, 1)
                        em("tensor", t_w1)
                        c_w1 = ctr["tp"].inc(1)

                        def a_gelu(sc, f=f, c_w1=c_w1):
                            sc.wait_ge(tp_h, c_w1)
                            sc.activation(h_sb[:, f, :], ps_h[f % 2], AF.Gelu,
                                          bias=b1_sb[:, f:f + 1]).then_inc(ap_h, 1)
                        em("scalar", a_gelu)
                        gelu_c.append(ctr["ap"].inc(1))
                    if ff >= 1:
                        f = ff - 1
                        def t_w2a(t, f=f, cg=gelu_c[f]):
                            t.wait_ge(ap_h, cg)
                            for d in range(4):
                                mm = t.matmul(ps_d[d],
                                              lhsT=w2s[:, f, d * 128:(d + 1) * 128],
                                              rhs=h_sb[:, f, :],
                                              start=(f == 0), stop=(f == 31))
                            mm.then_inc(tp_h, 1)
                        em("tensor", t_w2a)
                        c_w2a = ctr["tp"].inc(1)

                def v_evA(v, c_w2a=c_w2a):
                    v.wait_ge(tp_h, c_w2a)
                    for d in range(4):
                        ins = v.tensor_scalar(ffn_sb[:, d, :], ps_d[d],
                                              b2h_sb[:, d:d + 1], None, ALU.add)
                    ins.then_inc(vp_h, 1)
                em("vector", v_evA)
                c_evA = ctr["vp"].inc(1)

                for f in range(32):
                    def t_w2b(t, f=f, c_evA=c_evA):
                        if f == 0:
                            t.wait_ge(vp_h, c_evA)
                        for d in range(4):
                            mm = t.matmul(ps_d[d],
                                          lhsT=w2s[:, f, (d + 4) * 128:(d + 5) * 128],
                                          rhs=h_sb[:, f, :],
                                          start=(f == 0), stop=(f == 31))
                        mm.then_inc(tp_h, 1)
                    em("tensor", t_w2b)
                    c_w2b = ctr["tp"].inc(1)
                prev_h_free = c_w2b

                def v_evB(v, c_w2b=c_w2b):
                    v.wait_ge(tp_h, c_w2b)
                    for d in range(4):
                        ins = v.tensor_scalar(ffn_sb[:, d + 4, :], ps_d[d],
                                              b2h_sb[:, d + 4:d + 5], None, ALU.add)
                    ins.then_inc(vp_h, 1)
                em("vector", v_evB)
                c_evB = ctr["vp"].inc(1)
                prev_evB = c_evB

                store_cs = []
                res_cs = []
                for tt in range(4):
                    def t_ftr(t, tt=tt, c_evB=c_evB,
                              pr=res_cs[-2] if len(res_cs) >= 2 else None):
                        if tt == 0:
                            t.wait_ge(vp_h, c_evB)
                        if pr is not None:
                            t.wait_ge(vp_h, pr)
                        for dc in range(8):
                            mm = t.transpose(ps_tr01[:, (tt % 2) * 1024 + dc * 128:(tt % 2) * 1024 + (dc + 1) * 128],
                                             ffn_sb[:, dc, tt * 128:(tt + 1) * 128],
                                             ident_sb[:, :])
                        mm.then_inc(tp_h, 1)
                    em("tensor", t_ftr)
                    c_ftr = ctr["tp"].inc(1)

                    def v_res(v, tt=tt, c_ftr=c_ftr):
                        v.wait_ge(tp_h, c_ftr)
                        v.tensor_tensor(x_sb[:, tt, :], x_sb[:, tt, :],
                                        ps_tr01[:, (tt % 2) * 1024:(tt % 2) * 1024 + 1024], ALU.add).then_inc(vp_h, 1)
                    em("vector", v_res)
                    c_res = ctr["vp"].inc(1)
                    res_cs.append(c_res)

                    if out_shared:
                        def s_store(s, tt=tt, base=base, c_res=c_res):
                            s.wait_ge(vp_h, c_res)
                            with s.If_eq(r_role_s, 0):
                                s.dma_start(out=sh_x1[base + tt * 128: base + (tt + 1) * 128, :],
                                            in_=x_sb[:, tt, :]).then_inc(sq_h, 16)
                            with s.Else():
                                s.dma_start(out=sh_x1[HALF + base + tt * 128: HALF + base + (tt + 1) * 128, :],
                                            in_=x_sb[:, tt, :]).then_inc(sq_h, 16)
                    else:
                        def s_store(s, tt=tt, base=base, c_res=c_res):
                            s.wait_ge(vp_h, c_res)
                            s.dma_start(out=y_out[base + tt * 128: base + (tt + 1) * 128, :],
                                        in_=x_sb[:, tt, :]).then_inc(sq_h, 16)
                    em("sync", s_store)
                    store_cs.append(ctr["sq"].inc(16))
                prev_store = store_cs[-1]
                prev_res_last = res_cs[-1]
            return prev_store

    if stage != "p1b_only":
        c_tern_done = ctr["vp"].v
        c_p1a_done = ffn_phase(x_prod, NST1, True, w1t_sb, w2t_sb,
                               first_load_vp=c_tern_done)

        w2_ctx.__exit__(None, None, None)
        w1_ctx.__exit__(None, None, None)


    if stage == "p1a":
        return _finalize()
    if stage == "p1b_only":
        pass

    with nc.sbuf_tensor("q_w", [128, 8, 3 * C], BF16) as qkv_sb, \
         nc.sbuf_tensor("q_x", [128, 2, 4, C], F32) as x1_sb, \
         nc.sbuf_tensor("q_stA", [128, 12], F32) as stA, \
         nc.sbuf_tensor("q_stB", [128, 4], F32) as stB, \
         nc.sbuf_tensor("q_stC", [128, 4], F32) as stC, \
         nc.sbuf_tensor("q_ln", [128, 2, C], BF16) as ln_sb, \
         nc.sbuf_tensor("q_lnT", [128, 8, ST], BF16) as lnT_sb, \
         nc.sbuf_tensor("q_qst", [128, 2, 8, ST], BF16) as qst_sb, \
         nc.sbuf_tensor("q_kst", [128, 2, 8, ST], BF16) as kst_sb, \
         nc.sbuf_tensor("q_vst", [128, 2, 4, C], BF16) as vst_sb, \
         nc.psum_tensor("q_ptr01", [128, 2048], BF16) as ps_tr01, \
         nc.psum_tensor("q_pq01", [128, 1024], F32) as ps_q01, \
         nc.psum_tensor("q_pv4", [128, 2048], F32) as ps_v4:
        ps_tr = [ps_tr01[:, 0:1024], ps_tr01[:, 1024:2048]]
        ps_qk = [ps_q01[:, 0:512], ps_q01[:, 512:1024]]
        ps_v = [[ps_v4[:, 0:512], ps_v4[:, 512:1024]],
                [ps_v4[:, 1024:1536], ps_v4[:, 1536:2048]]]

        def g_qkvw(g, cq=c_p1a_done):
            g.wait_ge(sq_h, cq)
            g.dma_start(out=qkv_sb[:, :, :],
                        in_=qkvt_in.rearrange("(cc p) f -> p cc f", p=128)
                        ).then_inc(sw_h, 16)
        em("gpsimd", g_qkvw)
        c_qkvw = ctr["sw"].inc(16)

        prev_store = c_p1a_done
        prev_evac_a = None
        tr_marks = []
        stats_marks = []
        qk_marks = []
        vev_marks = []
        ln_marks = []
        store_marks = [[], []]
        for st_i in range(NST1):
            base = st_i * ST
            def s_x1(s, base=base, st_i=st_i,
                     pln=ln_marks[-5] if len(ln_marks) >= 5 else None,
                     pv0=c_p1a_done if st_i == 0 else None):
                if pln is not None:
                    s.wait_ge(vp_h, pln)
                if pv0 is not None:
                    s.wait_ge(sq_h, pv0)
                sh_r = sh_x1.rearrange("(n p) c -> p n c", p=128)
                with s.If_eq(r_role_s, 0):
                    s.dma_start(out=x1_sb[:, st_i % 2, :, :],
                                in_=sh_r[:, base // 128:base // 128 + 4, :]
                                ).then_inc(sq_h, 16)
                with s.Else():
                    s.dma_start(out=x1_sb[:, st_i % 2, :, :],
                                in_=sh_r[:, (HALF + base) // 128:(HALF + base) // 128 + 4, :]
                                ).then_inc(sq_h, 16)
            em("sync", s_x1)
            c_x = ctr["sq"].inc(16)
            c_xt = [c_x] * 4
            lnT_ready = None
            for tt in range(4):
                ptr = tr_marks[-2] if len(tr_marks) >= 2 else None
                psv = stats_marks[-1] if stats_marks else None
                c_ln, c_var = emit_ln(x1_sb[:, st_i % 2, tt, :], stA, stB, stC,
                                      ln_sb[:, tt % 2, :], c_xt[tt], ptr, xr_sem=sq_h)
                stats_marks.append(c_var)
                ln_marks.append(c_ln)
                cev, c_tr = emit_transposes(
                    lambda cc, tt=tt: ln_sb[:, tt % 2, cc * 128:(cc + 1) * 128],
                    8, ps_tr[tt % 2],
                    lambda cc, tt=tt: lnT_sb[:, cc, tt * 128:(tt + 1) * 128],
                    c_ln)
                tr_marks.append(c_tr)
                lnT_ready = cev
            for which in ("q", "k"):
                colbase = 0 if which == "q" else C
                scale = (1.0 / 32.0) if which == "q" else 1.0
                dstst = qst_sb if which == "q" else kst_sb
                for fc in range(8):
                    def t_p(t, fc=fc, colbase=colbase, which=which,
                            lnr=lnT_ready if (fc == 0 and which == "q") else None,
                            pea=qk_marks[-2] if len(qk_marks) >= 2 else None):
                        if lnr is not None:
                            t.wait_ge(ap_h, lnr)
                            t.wait_ge(sw_h, c_qkvw)
                        if pea is not None:
                            t.wait_ge(ap_h, pea)
                        for cc in range(8):
                            mm = t.matmul(ps_qk[fc % 2],
                                          lhsT=qkv_sb[:, cc, colbase + fc * 128: colbase + (fc + 1) * 128],
                                          rhs=lnT_sb[:, cc, :],
                                          start=(cc == 0), stop=(cc == 7))
                        mm.then_inc(tp_h, 1)
                    em("tensor", t_p)
                    c_mm = ctr["tp"].inc(1)

                    def a_ev(sc, fc=fc, dstst=dstst, scale=scale, c_mm=c_mm,
                             which=which, st_i=st_i,
                             pstore=(store_marks[st_i % 2][-1] if store_marks[st_i % 2] else None)
                             if (fc == 0 and which == "q") else None):
                        sc.wait_ge(tp_h, c_mm)
                        if pstore is not None:
                            sc.wait_ge(qst_h[st_i % 2], pstore)
                        sc.activation(dstst[:, st_i % 2, fc, :], ps_qk[fc % 2],
                                      AF.Copy, scale=scale).then_inc(ap_h, 1)
                    em("scalar", a_ev)
                    prev_evac_a = ctr["ap"].inc(1)
                    qk_marks.append(prev_evac_a)
            for tt in range(4):
                def t_v(t, tt=tt,
                        pea=prev_evac_a if tt == 0 else None,
                        pvv=vev_marks[-2] if len(vev_marks) >= 2 else None):
                    if pea is not None:
                        t.wait_ge(ap_h, pea)
                    if pvv is not None:
                        t.wait_ge(ap_h, pvv)
                    for half in range(2):
                        for cc in range(8):
                            mm = t.matmul(ps_v[tt % 2][half],
                                          lhsT=lnT_sb[:, cc, tt * 128:(tt + 1) * 128],
                                          rhs=qkv_sb[:, cc, 2 * C + half * 512: 2 * C + (half + 1) * 512],
                                          start=(cc == 0), stop=(cc == 7))
                    mm.then_inc(tp_h, 1)
                em("tensor", t_v)
                c_vm = ctr["tp"].inc(1)

                def a_vev(sc, tt=tt, c_vm=c_vm, st_i=st_i):
                    sc.wait_ge(tp_h, c_vm)
                    for half in range(2):
                        ins = sc.copy(vst_sb[:, st_i % 2, tt, half * 512:(half + 1) * 512],
                                      ps_v[tt % 2][half])
                    ins.then_inc(ap_h, 1)
                em("scalar", a_vev)
                prev_evac_a = ctr["ap"].inc(1)
                vev_marks.append(prev_evac_a)
            c_allev = prev_evac_a

            def s_qkvout(s, base=base, c_allev=c_allev, st_i=st_i):
                s.wait_ge(ap_h, c_allev)
                with s.If_eq(r_role_s, 0):
                    s.dma_start(out=sh_qT.rearrange("(cc p) t -> p cc t", p=128)[:, :, base:base + ST],
                                in_=qst_sb[:, st_i % 2, :, :]).then_inc(qst_h[st_i % 2], 16)
                    s.dma_start(out=sh_kT.rearrange("(cc p) t -> p cc t", p=128)[:, :, base:base + ST],
                                in_=kst_sb[:, st_i % 2, :, :]).then_inc(qst_h[st_i % 2], 16)
                    s.dma_start(out=sh_v.rearrange("(n p) c -> p n c", p=128)[:, base // 128:base // 128 + 4, :],
                                in_=vst_sb[:, st_i % 2, :, :]).then_inc(qst_h[st_i % 2], 16)
                with s.Else():
                    s.dma_start(out=sh_qT.rearrange("(cc p) t -> p cc t", p=128)[:, :, HALF + base:HALF + base + ST],
                                in_=qst_sb[:, st_i % 2, :, :]).then_inc(qst_h[st_i % 2], 16)
                    s.dma_start(out=sh_kT.rearrange("(cc p) t -> p cc t", p=128)[:, :, HALF + base:HALF + base + ST],
                                in_=kst_sb[:, st_i % 2, :, :]).then_inc(qst_h[st_i % 2], 16)
                    s.dma_start(out=sh_v.rearrange("(n p) c -> p n c", p=128)[:, (HALF + base) // 128:(HALF + base) // 128 + 4, :],
                                in_=vst_sb[:, st_i % 2, :, :]).then_inc(qst_h[st_i % 2], 16)
            em("sync", s_qkvout)
            prev_store = ctr["qst%d" % (st_i % 2)].inc(48)
            store_marks[st_i % 2].append(prev_store)
        c_p1b_q0 = ctr["qst0"].v
        c_p1b_q1 = ctr["qst1"].v

    if stage in ("p1b", "p1b_only"):
        return _finalize()

    c_sq_prebar = ctr["sq"].v

    def g_barrier(g, q0=c_p1b_q0, q1=c_p1b_q1, c_sq_prebar=c_sq_prebar):
        g.wait_ge(qst_h[0], q0)
        g.wait_ge(qst_h[1], q1)
        g.wait_ge(sq_h, c_sq_prebar)
        g.reg_load(r_exp_g, meta_sb[:1, 1:2])
        g.reg_save(flag_sb[:1, :1], r_exp_g)
        g.reg_mov(r_tmp_g, 0)
        g.reg_add(r_tmp_g, r_role_g, 0)
        g.reg_mul(r_tmp_g, r_tmp_g, 16)
        g.dma_start(bass.AP(sh_flag.tensor, r_tmp_g, [[1, 1], [1, 1]]),
                    flag_sb[:1, :1]).then_inc(pq_h, 16)
        g.reg_mov(r_sc_g, 16)
        g.wait_ge(pq_h, r_sc_g)
        g.reg_load(r_exp_g, meta_sb[:1, 2:3])
        g.reg_mov(r_tmp_g, 1)
        g.reg_sub(r_tmp_g, r_tmp_g, r_role_g)
        g.reg_mul(r_tmp_g, r_tmp_g, 16)
        g.reg_mov(r_diff_g, 1)
        with g.While(r_diff_g):
            g.dma_start(flag_sb[:1, :1],
                        bass.AP(sh_flag.tensor, r_tmp_g, [[1, 1], [1, 1]])).then_inc(pq_h, 16)
            g.reg_add(r_sc_g, r_sc_g, 16)
            g.wait_ge(pq_h, r_sc_g)
            g.reg_load(r_val_g, flag_sb[:1, :1])
            g.reg_sub(r_diff_g, r_val_g, r_exp_g)
        g.engine_nop().then_inc(bar_h, 1)
    em("gpsimd", g_barrier)
    c_bar = ctr["bar"].inc(1)

    if stage == "bar":
        def g_barfin(g):
            g.wait_ge(bar_h, c_bar)
        em("gpsimd", g_barfin)
        return _finalize()

    with nc.sbuf_tensor("a_ow", [128, 8, C], BF16) as outw_sb, \
         nc.sbuf_tensor("a_k", [128, 2, 8, 1024], BF16) as k_sb, \
         nc.sbuf_tensor("a_v", [128, 2, 8, 1024], BF16) as v_sb, \
         nc.sbuf_tensor("a_q", [128, 2, 8, 256], BF16) as q_sb, \
         nc.sbuf_tensor("a_x1", [128, 2, 2, C], F32) as x1s_sb, \
         nc.sbuf_tensor("a_pt", [128, 8, 256], BF16) as pt_sb, \
         nc.sbuf_tensor("a_o", [128, 2, C], F32) as o_sb, \
         nc.sbuf_tensor("a_l", [128, 8], F32) as l_sb, \
         nc.sbuf_tensor("a_linv", [128, 2], F32) as linv_sb, \
         nc.sbuf_tensor("a_at", [128, 2, C], BF16) as attn_sb, \
         nc.sbuf_tensor("a_att", [128, 8, 256], BF16) as attnT_sb, \
         nc.sbuf_tensor("a_y", [128, 8, 256], BF16) as y_sb, \
         nc.sbuf_tensor("a_x2", [128, 2, C], F32) as x2_sb, \
         nc.psum_tensor("a_ps01", [128, 1024], F32) as ps_s01, \
         nc.psum_tensor("a_po01", [128, 2048], F32) as ps_o01, \
         nc.psum_tensor("a_pl", [128, 512], F32) as ps_l, \
         nc.psum_tensor("a_ptb", [128, 1024], BF16) as ps_tb:
        ps_s = [ps_s01[:, 0:512], ps_s01[:, 512:1024]]
        ps_o = [ps_o01[:, 0:1024], ps_o01[:, 1024:2048]]

        def g_outw(g, q0=c_p1b_q0, q1=c_p1b_q1):
            g.wait_ge(qst_h[0], q0)
            g.wait_ge(qst_h[1], q1)
            g.dma_start(out=outw_sb[:, :, :],
                        in_=outt_in.rearrange("(cc p) d -> p cc d", p=128)
                        ).then_inc(sw_h, 16)
        em("gpsimd", g_outw)
        c_outw = ctr["sw"].inc(16)

        exp_marks = []
        kv_user = {0: None, 1: None}
        q_user = {0: None, 1: None}
        x1_user = {0: None, 1: None}
        oflush = {0: None, 1: None}
        lflush = None
        last_x2_store = None
        prev_psS_user = None
        c_av_last = None
        for k in range(NSLOT):
            nsb = k + 1
            kbuf = k % 2

            def g_qload(g, k=k, kbuf=kbuf, first=(k == 0),
                        qu=q_user[kbuf], xu=x1_user[kbuf]):
                if first:
                    g.wait_ge(bar_h, c_bar)
                if qu is not None:
                    g.wait_ge(tp_h, qu)
                if xu is not None:
                    g.wait_ge(vp_h, xu)

                def _loads(qv):
                    s0 = (4 * k + qv) * 256
                    g.dma_start(out=q_sb[:, kbuf, :, :],
                                in_=sh_qT.rearrange("(cc p) t -> p cc t", p=128)
                                [:, :, s0:s0 + 256]).then_inc(gqq_h[kbuf], 16)
                    g.dma_start(out=x1s_sb[:, kbuf, :, :],
                                in_=sh_x1.rearrange("(n p) c -> p n c", p=128)
                                [:, s0 // 128:s0 // 128 + 2, :]).then_inc(gqq_h[kbuf], 16)
                with g.If_eq(r_q_g, 0):
                    _loads(0)
                with g.Else():
                    with g.If_eq(r_q_g, 1):
                        _loads(1)
                    with g.Else():
                        with g.If_eq(r_q_g, 2):
                            _loads(2)
                        with g.Else():
                            _loads(3)
            em("gpsimd", g_qload)
            c_qld = ctr["gqq%d" % kbuf].inc(32)

            flushes = []
            for j in range(nsb):
                jbuf = j % 2

                def s_kv(s, j=j, jbuf=jbuf, first=(k == 0 and j == 0),
                         ku=kv_user[jbuf]):
                    if first:
                        s.wait_ge(bar_h, c_bar)
                    if ku is not None:
                        s.wait_ge(tp_h, ku)
                    s.dma_start(out=k_sb[:, jbuf, :, :],
                                in_=sh_kT.rearrange("(cc p) t -> p cc t", p=128)
                                [:, :, j * 1024:(j + 1) * 1024]).then_inc(skv_h[jbuf], 16)
                    s.dma_start(out=v_sb[:, jbuf, :, :],
                                in_=sh_v.rearrange("(n p) c -> p n c", p=128)
                                [:, j * 8:(j + 1) * 8, :]).then_inc(skv_h[jbuf], 16)
                em("sync", s_kv)
                c_kv = ctr["skv%d" % jbuf].inc(32)

                for kt in range(8):
                    def t_qk(t, kt=kt, jbuf=jbuf, c_kv=c_kv, c_qld=c_qld,
                             kbuf=kbuf,
                             pw=exp_marks[-2] if len(exp_marks) >= 2 else None,
                             psu=prev_psS_user if (j == 0 and kt == 0) else None):
                        if kt == 0:
                            t.wait_ge(skv_h[jbuf], c_kv)
                            t.wait_ge(gqq_h[kbuf], c_qld)
                            if psu is not None:
                                t.wait_ge(vp_h, psu)
                        if pw is not None:
                            t.wait_ge(ap_h, pw)
                        for cc in range(8):
                            mm = t.matmul(ps_s[kt % 2][:, 0:256],
                                          lhsT=k_sb[:, jbuf, cc, kt * 128:(kt + 1) * 128],
                                          rhs=q_sb[:, kbuf, cc, :],
                                          start=(cc == 0), stop=(cc == 7))
                        mm.then_inc(tp_h, 1)
                    em("tensor", t_qk)
                    c_qk = ctr["tp"].inc(1)

                    def a_exp(sc, kt=kt, c_qk=c_qk):
                        sc.wait_ge(tp_h, c_qk)
                        sc.activation(pt_sb[:, kt, :], ps_s[kt % 2][:, 0:256],
                                      AF.Exp).then_inc(ap_h, 1)
                    em("scalar", a_exp)
                    exp_marks.append(ctr["ap"].inc(1))

                c_mask = None
                if j == nsb - 1:
                    def v_mask(v, ce=exp_marks[-1]):
                        v.wait_ge(ap_h, ce)
                        for kt in range(8):
                            ins = v.tensor_tensor(pt_sb[:, kt, :], pt_sb[:, kt, :],
                                                  mask_sb[:, kt, :], ALU.mult)
                        ins.then_inc(vp_h, 1)
                    em("vector", v_mask)
                    c_mask = ctr["vp"].inc(1)

                for qs in range(2):
                    def t_av(t, qs=qs, jbuf=jbuf, j=j, ce=exp_marks[-1],
                             cm=c_mask,
                             ofl=oflush[qs], lfl=lflush if qs == 0 else None):
                        if cm is not None:
                            t.wait_ge(vp_h, cm)
                        if ofl is not None:
                            t.wait_ge(vp_h, ofl)
                        if lfl is not None:
                            t.wait_ge(vp_h, lfl)
                        for kt in range(8):
                            st = (kt == 0)
                            t.wait_ge(ap_h, ce - 7 + kt)
                            t.matmul(ps_o[qs][:, 0:512],
                                     lhsT=pt_sb[:, kt, qs * 128:(qs + 1) * 128],
                                     rhs=v_sb[:, jbuf, kt, 0:512],
                                     start=st, stop=False)
                            t.matmul(ps_o[qs][:, 512:1024],
                                     lhsT=pt_sb[:, kt, qs * 128:(qs + 1) * 128],
                                     rhs=v_sb[:, jbuf, kt, 512:1024],
                                     start=st, stop=False)
                            mm = t.matmul(ps_l[:, qs * 8:qs * 8 + 1],
                                          lhsT=pt_sb[:, kt, qs * 128:(qs + 1) * 128],
                                          rhs=ones_sb[:, :],
                                          start=st, stop=(kt == 7))
                        mm.then_inc(tp_h, 1)
                    em("tensor", t_av)
                    c_av = ctr["tp"].inc(1)
                    c_av_last = c_av
                    kv_user[jbuf] = c_av
                    q_user[kbuf] = c_av

                    def v_oflush(v, qs=qs, j=j, c_av=c_av):
                        v.wait_ge(tp_h, c_av)
                        if j == 0:
                            v.tensor_copy(o_sb[:, qs, :], ps_o[qs]).then_inc(vp_h, 1)
                        else:
                            v.tensor_tensor(o_sb[:, qs, :], o_sb[:, qs, :],
                                            ps_o[qs], ALU.add).then_inc(vp_h, 1)
                    em("vector", v_oflush)
                    oflush[qs] = ctr["vp"].inc(1)
                    flushes.append(oflush[qs])

                def v_lflush(v, j=j, c_av=c_av_last):
                    v.wait_ge(tp_h, c_av)
                    if j == 0:
                        v.tensor_copy(l_sb[:, 0:1], ps_l[:, 0:1])
                        v.tensor_copy(l_sb[:, 1:2], ps_l[:, 8:9]).then_inc(vp_h, 1)
                    else:
                        v.tensor_tensor(l_sb[:, 0:1], l_sb[:, 0:1], ps_l[:, 0:1], ALU.add)
                        v.tensor_tensor(l_sb[:, 1:2], l_sb[:, 1:2], ps_l[:, 8:9],
                                        ALU.add).then_inc(vp_h, 1)
                em("vector", v_lflush)
                lflush = ctr["vp"].inc(1)

            base_nm = ctr["vp"].v

            def v_norm(v, cf=lflush, co0=oflush[0], co1=oflush[1], base=base_nm):
                v.wait_ge(vp_h, max(cf, co0, co1))
                v.reciprocal(linv_sb[:, 0:2], l_sb[:, 0:2]).then_inc(vp_h, 1)
                v.wait_ge(vp_h, base + 1)
                for qs in range(2):
                    ins = v.tensor_scalar(attn_sb[:, qs, :], o_sb[:, qs, :],
                                          linv_sb[:, qs:qs + 1], None, ALU.mult)
                ins.then_inc(vp_h, 1)
            em("vector", v_norm)
            c_norm = ctr["vp"].inc(2)

            ev_c = None
            for qs in range(2):
                def t_atr(t, qs=qs, c_norm=c_norm, ev_c=ev_c):
                    if qs == 0:
                        t.wait_ge(vp_h, c_norm)
                    if ev_c is not None:
                        t.wait_ge(ap_h, ev_c)
                    for cc in range(8):
                        mm = t.transpose(ps_tb[:, cc * 128:(cc + 1) * 128],
                                         attn_sb[:, qs, cc * 128:(cc + 1) * 128],
                                         ident_sb[:, :])
                    mm.then_inc(tp_h, 1)
                em("tensor", t_atr)
                c_atr = ctr["tp"].inc(1)

                def a_aev(sc, qs=qs, c_atr=c_atr):
                    sc.wait_ge(tp_h, c_atr)
                    for cc in range(8):
                        ins = sc.copy(attnT_sb[:, cc, qs * 128:(qs + 1) * 128],
                                      ps_tb[:, cc * 128:(cc + 1) * 128])
                    ins.then_inc(ap_h, 1)
                em("scalar", a_aev)
                ev_c = ctr["ap"].inc(1)

            proj_v = []
            for dc in range(8):
                def t_proj(t, dc=dc, ev_c=ev_c, c_outw=c_outw,
                           pv=proj_v[-2] if len(proj_v) >= 2 else None):
                    if dc == 0:
                        t.wait_ge(ap_h, ev_c)
                        t.wait_ge(sw_h, c_outw)
                    if pv is not None:
                        t.wait_ge(vp_h, pv)
                    for cc in range(8):
                        mm = t.matmul(ps_s[dc % 2][:, 0:256],
                                      lhsT=outw_sb[:, cc, dc * 128:(dc + 1) * 128],
                                      rhs=attnT_sb[:, cc, :],
                                      start=(cc == 0), stop=(cc == 7))
                    mm.then_inc(tp_h, 1)
                em("tensor", t_proj)
                c_pj = ctr["tp"].inc(1)

                def v_pev(v, dc=dc, c_pj=c_pj):
                    v.wait_ge(tp_h, c_pj)
                    v.tensor_scalar(y_sb[:, dc, :], ps_s[dc % 2][:, 0:256],
                                    outb_sb[:, dc:dc + 1], None, ALU.add).then_inc(vp_h, 1)
                em("vector", v_pev)
                proj_v.append(ctr["vp"].inc(1))
            prev_psS_user = proj_v[-1]

            res_vs = []
            for qs in range(2):
                def t_ytr(t, qs=qs, pv=proj_v[-1],
                          pr=res_vs[-1] if res_vs else None):
                    if qs == 0:
                        t.wait_ge(vp_h, pv)
                    if pr is not None:
                        t.wait_ge(vp_h, pr)
                    for dc in range(8):
                        mm = t.transpose(ps_tb[:, dc * 128:(dc + 1) * 128],
                                         y_sb[:, dc, qs * 128:(qs + 1) * 128],
                                         ident_sb[:, :])
                    mm.then_inc(tp_h, 1)
                em("tensor", t_ytr)
                c_ytr = ctr["tp"].inc(1)

                def v_x2(v, qs=qs, k=k, kbuf=kbuf, c_ytr=c_ytr,
                         lxs=last_x2_store if qs == 0 else None):
                    v.wait_ge(tp_h, c_ytr)
                    if lxs is not None:
                        v.wait_ge(sq_h, lxs)
                    v.tensor_tensor(x2_sb[:, qs, :], x1s_sb[:, kbuf, qs, :],
                                    ps_tb[:, :], ALU.add).then_inc(vp_h, 1)
                em("vector", v_x2)
                res_vs.append(ctr["vp"].inc(1))
            x1_user[kbuf] = res_vs[-1]

            def s_x2(s, k=k, cr=res_vs[-1]):
                s.wait_ge(vp_h, cr)
                s.dma_start(out=x2_dram.rearrange("(n p) c -> p n c", p=128)
                            [:, k * 2:k * 2 + 2, :],
                            in_=x2_sb[:, :, :]).then_inc(sq_h, 16)
            em("sync", s_x2)
            last_x2_store = ctr["sq"].inc(16)

    if stage == "p2":
        return _finalize()

    w1_ctx2 = nc.sbuf_tensor("w1b_sb", [128, 8, FF], BF16)
    w2_ctx2 = nc.sbuf_tensor("w2b_sb", [128, 32, C], BF16)
    w1t_sb = w1_ctx2.__enter__()
    w2t_sb = w2_ctx2.__enter__()

    def s_wreload(s):
        s.dma_start(out=w1t_sb[:, :, :],
                    in_=w1t_st.rearrange("(cc p) f -> p cc f", p=128)).then_inc(sw_h, 16)
        s.dma_start(out=w2t_sb[:, :, :],
                    in_=w2t_st.rearrange("(fc p) d -> p fc d", p=128)).then_inc(sw_h, 16)
    em("sync", s_wreload)
    c_wrl = ctr["sw"].inc(32)

    ffn_phase(x2_dram, QROWS // ST, False, w1t_sb, w2t_sb,
              first_tr_sq=c_wrl, first_load_sq=last_x2_store, tag="b")

    w2_ctx2.__exit__(None, None, None)
    w1_ctx2.__exit__(None, None, None)

    return _finalize()



def kernel(L0_attn_x, L0_attn_qkv_w, L0_attn_out_w, L0_attn_out_b,
           L0_ffn_W1, L0_ffn_b1, L0_ffn_W2, L0_ffn_b2, _debug=False, _stage="all"):
    x = np.ascontiguousarray(np.asarray(L0_attn_x, dtype=np.float32))
    w1t = np.ascontiguousarray(np.asarray(L0_ffn_W1, np.float32).T)
    w2t = np.ascontiguousarray(np.asarray(L0_ffn_W2, np.float32).T)
    qkvt = np.ascontiguousarray(np.asarray(L0_attn_qkv_w, np.float32).T)
    outt = np.ascontiguousarray(np.asarray(L0_attn_out_w, np.float32).T)
    b1r = np.ascontiguousarray(np.asarray(L0_ffn_b1, np.float32).reshape(FF // 128, 128).T)
    b2r = np.ascontiguousarray(np.asarray(L0_ffn_b2, np.float32).reshape(C // 128, 128).T)
    outbr = np.ascontiguousarray(np.asarray(L0_attn_out_b, np.float32).reshape(C // 128, 128).T)
    ident = np.eye(128, dtype=ml_dtypes.bfloat16)

    nonces = np.random.randint(1, 2 ** 30, size=NCORES).astype(np.int32)
    in_maps = []
    for c in range(NCORES):
        b = c // 4
        q = c % 4
        role = c % 2
        xp = np.ascontiguousarray(x[b, role * HALF:(role + 1) * HALF, :])
        i_idx = np.arange(128)[:, None]
        j_idx = np.arange(256)[None, :]
        mask = np.zeros((128, 8, 256), dtype=ml_dtypes.bfloat16)
        for t in range(8):
            mask[:, t, :] = (t * 128 + i_idx <= q * 256 + j_idx).astype(ml_dtypes.bfloat16)
        meta = np.array([[role, nonces[c], nonces[c ^ 1], q, 0, 0, 0, 0]], dtype=np.int32)
        in_maps.append({
            "x_prod": xp, "w1t_in": w1t, "w2t_in": w2t, "qkvt_in": qkvt,
            "outt_in": outt, "b1_in": b1r, "b2_in": b2r, "outb_in": outbr,
            "ident_in": ident, "mask_in": mask, "meta_in": meta,
        })

    nc = build_program(debug_taps=_debug, stage=_stage)
    res = run_bass_kernel_spmd(nc, in_maps, core_ids=list(range(NCORES)))
    if _debug:
        kernel._debug_res = res

    out = np.empty((B, T, C), dtype=np.float32)
    for c in range(NCORES):
        b = c // 4
        q = c % 4
        y = res.results[c]["y_out"]
        for k in range(NSLOT):
            s = 4 * k + q
            out[b, s * 256:(s + 1) * 256, :] = y[k * 256:(k + 1) * 256, :]
    return out


# version 07 (final)
# speedup vs baseline: 1.2104x; 1.2104x over previous
import numpy as np
import ml_dtypes
import concourse.bass as bass
import concourse.mybir as mybir
from concourse.bass_utils import run_bass_kernel_spmd

F32 = mybir.dt.float32
BF16 = mybir.dt.bfloat16
I32 = mybir.dt.int32
AF = mybir.ActivationFunctionType
ALU = mybir.AluOpType
AX = mybir.AxisListType
ET = mybir.EngineType

B = 2
T = 16384
C = 1024
FF = 4096
NCORES = 8
HALF = T // 2
ST = 512
NST1 = HALF // ST
NSLOT = 16
QROWS = NSLOT * 256
RECIP_N = 1.0 / float(C)
EPS = 1e-5
MAGIC = 192.0


class Ctr:
    def __init__(self):
        self.v = 0

    def inc(self, n):
        self.v += n
        return self.v


class _SemProxy:
    def __init__(self, hmap, name):
        self.hmap = hmap
        self.name = name

    def __getattr__(self, a):
        return getattr(self.hmap[self.name], a)


def build_program(debug_taps=False, detect_races=True, stage="all"):
    nc = bass.Bass("TRN2", target_bir_lowering=False,
                   detect_race_conditions=detect_races)

    x_prod = nc.dram_tensor("x_prod", [HALF, C], F32, kind="ExternalInput").ap()
    w1t_in = nc.dram_tensor("w1t_in", [C, FF], F32, kind="ExternalInput").ap()
    w2t_in = nc.dram_tensor("w2t_in", [FF, C], F32, kind="ExternalInput").ap()
    qkvt_in = nc.dram_tensor("qkvt_in", [C, 3 * C], F32, kind="ExternalInput").ap()
    outt_in = nc.dram_tensor("outt_in", [C, C], F32, kind="ExternalInput").ap()
    b1_in = nc.dram_tensor("b1_in", [128, FF // 128], F32, kind="ExternalInput").ap()
    b2_in = nc.dram_tensor("b2_in", [128, C // 128], F32, kind="ExternalInput").ap()
    outb_in = nc.dram_tensor("outb_in", [128, C // 128], F32, kind="ExternalInput").ap()
    ident_in = nc.dram_tensor("ident_in", [128, 128], BF16, kind="ExternalInput").ap()
    mask_in = nc.dram_tensor("mask_in", [128, 8, 256], BF16, kind="ExternalInput").ap()
    meta_in = nc.dram_tensor("meta_in", [1, 8], I32, kind="ExternalInput").ap()
    y_out = nc.dram_tensor("y_out", [QROWS, C], F32, kind="ExternalOutput").ap()
    if debug_taps:
        d_x1 = nc.dram_tensor("d_x1", [T, C], F32, kind="ExternalOutput").ap()
        d_q = nc.dram_tensor("d_q", [C, T], F32, kind="ExternalOutput").ap()
        d_k = nc.dram_tensor("d_k", [C, T], F32, kind="ExternalOutput").ap()
        d_v = nc.dram_tensor("d_v", [T, C], F32, kind="ExternalOutput").ap()
        d_x2 = nc.dram_tensor("d_x2", [QROWS, C], F32, kind="ExternalOutput").ap()
        d_w1t = nc.dram_tensor("d_w1t", [C, FF], F32, kind="ExternalOutput").ap()
        d_scal = nc.dram_tensor("d_scal", [128, 4], F32, kind="ExternalOutput").ap()

    sh_x1 = nc.dram_tensor("sh_x1", [T, C], F32, kind="Internal", addr_space="Shared").ap()
    sh_kT = nc.dram_tensor("sh_kT", [C, T], BF16, kind="Internal", addr_space="Shared").ap()
    sh_qT = nc.dram_tensor("sh_qT", [C, T], BF16, kind="Internal", addr_space="Shared").ap()
    sh_v = nc.dram_tensor("sh_v", [T, C], BF16, kind="Internal", addr_space="Shared").ap()
    sh_flag = nc.dram_tensor("sh_flag", [2, 16], I32, kind="Internal", addr_space="Shared").ap()
    w1t_st = nc.dram_tensor("w1t_st", [C, FF], BF16, kind="Internal").ap()
    w2t_st = nc.dram_tensor("w2t_st", [FF, C], BF16, kind="Internal").ap()
    x2_dram = nc.dram_tensor("x2_dram", [QROWS, C], F32, kind="Internal").ap()

    ops = {k: [] for k in ("sync", "scalar", "vector", "tensor", "gpsimd")}

    def em(eng, fn):
        if eng == "vector":
            ops[eng].append((fn, ctr["vp"].v))
        elif eng == "scalar":
            ops[eng].append((fn, ctr["ap"].v))
        else:
            ops[eng].append((fn, None))

    ctr = {k: Ctr() for k in ("sq", "gq", "tp", "ap", "vp", "bar")}
    sq_h = nc.ctx.enter_context(nc.semaphore("sq"))
    gq_h = nc.ctx.enter_context(nc.semaphore("gq"))
    pq_h = nc.ctx.enter_context(nc.semaphore("pq"))
    tp_h = nc.ctx.enter_context(nc.semaphore("tp"))
    ap_h = nc.ctx.enter_context(nc.semaphore("ap"))
    vp_h = nc.ctx.enter_context(nc.semaphore("vp"))
    bar_h = nc.ctx.enter_context(nc.semaphore("bar"))
    sw_h = nc.ctx.enter_context(nc.semaphore("sw"))
    qst_h = [nc.ctx.enter_context(nc.semaphore("qst0")),
             nc.ctx.enter_context(nc.semaphore("qst1"))]
    skv_h = [nc.ctx.enter_context(nc.semaphore("skv0")),
             nc.ctx.enter_context(nc.semaphore("skv1"))]
    gqq_h = [nc.ctx.enter_context(nc.semaphore("gqq0")),
             nc.ctx.enter_context(nc.semaphore("gqq1"))]
    sx1_h = [nc.ctx.enter_context(nc.semaphore("sx10")),
             nc.ctx.enter_context(nc.semaphore("sx11"))]
    ctr["sw"] = Ctr()
    ctr["qst0"] = Ctr()
    ctr["qst1"] = Ctr()
    ctr["skv0"] = Ctr()
    ctr["skv1"] = Ctr()
    ctr["gqq0"] = Ctr()
    ctr["gqq1"] = Ctr()
    ctr["sx10"] = Ctr()
    ctr["sx11"] = Ctr()

    r_role_g = nc.alloc_register(ET.Pool, "r_role_g")
    r_tmp_g = nc.alloc_register(ET.Pool, "r_tmp_g")
    r_exp_g = nc.alloc_register(ET.Pool, "r_exp_g")
    r_val_g = nc.alloc_register(ET.Pool, "r_val_g")
    r_diff_g = nc.alloc_register(ET.Pool, "r_diff_g")
    r_sc_g = nc.alloc_register(ET.Pool, "r_sc_g")
    r_q_g = nc.alloc_register(ET.Pool, "r_q_g")
    r_off_g = nc.alloc_register(ET.Pool, "r_off_g")
    r_role_s = nc.alloc_register(ET.SP, "r_role_s")

    ctx = nc.ctx

    ident_sb = ctx.enter_context(nc.sbuf_tensor("ident_sb", [128, 128], BF16))
    b1_sb = ctx.enter_context(nc.sbuf_tensor("b1_sb", [128, FF // 128], F32))
    b2h_sb = ctx.enter_context(nc.sbuf_tensor("b2h_sb", [128, C // 128], F32))
    outb_sb = ctx.enter_context(nc.sbuf_tensor("outb_sb", [128, C // 128], F32))
    meta_sb = ctx.enter_context(nc.sbuf_tensor("meta_sb", [1, 8], I32))
    flag_sb = ctx.enter_context(nc.sbuf_tensor("flag_sb", [1, 1], I32))
    ones_sb = ctx.enter_context(nc.sbuf_tensor("ones_sb", [128, 1], BF16))
    onesf_sb = ctx.enter_context(nc.sbuf_tensor("onesf_sb", [128, 1], F32))
    onesrow_sb = ctx.enter_context(nc.sbuf_tensor("onesrow_sb", [1, 128], F32))
    mask_sb = ctx.enter_context(nc.sbuf_tensor("mask_sb", [128, 8, 256], BF16))
    scal_sb = ctx.enter_context(nc.sbuf_tensor("scal_sb", [128, 4], F32))
    eps_sb = ctx.enter_context(nc.sbuf_tensor("eps_sb", [128, 1], F32))

    def _emit_taps():
        if not debug_taps:
            return
        c_pre_taps = ctr["sq"].v

        c_pre_q0s = ctr["qst0"].v
        c_pre_q1s = ctr["qst1"].v

        def s_taps(s):
            s.wait_ge(sq_h, c_pre_taps)
            s.wait_ge(qst_h[0], c_pre_q0s)
            s.wait_ge(qst_h[1], c_pre_q1s)
            s.dma_start(out=d_x1, in_=sh_x1).then_inc(sq_h, 16)
            s.dma_start(out=d_x2, in_=x2_dram).then_inc(sq_h, 16)
        em("sync", s_taps)
        ctr["sq"].inc(32)

        c_tap_end = ctr["gq"].v + 80

        c_pre_q0 = ctr["qst0"].v
        c_pre_q1 = ctr["qst1"].v

        def g_taps(g, c_pre_taps=c_pre_taps, c_tap_end=c_tap_end,
                   c_pre_q0=c_pre_q0, c_pre_q1=c_pre_q1):
            g.wait_ge(sq_h, c_pre_taps)
            g.wait_ge(qst_h[0], c_pre_q0)
            g.wait_ge(qst_h[1], c_pre_q1)
            g.dma_start(out=d_q, in_=sh_qT).then_inc(gq_h, 16)
            g.dma_start(out=d_k, in_=sh_kT).then_inc(gq_h, 16)
            g.dma_start(out=d_v, in_=sh_v).then_inc(gq_h, 16)
            g.dma_start(out=d_w1t, in_=w1t_st).then_inc(gq_h, 16)
            g.dma_start(out=d_scal, in_=scal_sb[:, :]).then_inc(gq_h, 16)
            g.wait_ge(gq_h, c_tap_end)
        em("gpsimd", g_taps)
        ctr["gq"].inc(80)

    def _emit_blocks():
      with nc.Block() as block:
          @block.sync
          def _(s):
              for fn, _pre in ops["sync"]:
                  fn(s)

          @block.gpsimd
          def _(g):
              for fn, _pre in ops["gpsimd"]:
                  fn(g)

          @block.tensor
          def _(t):
              for fn, _pre in ops["tensor"]:
                  fn(t)

          @block.scalar
          def _(sc):
              for fn, pre in ops["scalar"]:
                  if pre:
                      sc.wait_ge(ap_h, pre)
                  fn(sc)

          @block.vector
          def _(v):
              for fn, pre in ops["vector"]:
                  if pre:
                      v.wait_ge(vp_h, pre)
                  fn(v)


    def _finalize():
        _emit_taps()
        c_all_sq = ctr["sq"].v
        c_all_q0 = ctr["qst0"].v
        c_all_q1 = ctr["qst1"].v

        def s_fin(s):
            s.wait_ge(sq_h, c_all_sq)
            s.wait_ge(qst_h[0], c_all_q0)
            s.wait_ge(qst_h[1], c_all_q1)
        em("sync", s_fin)
        _emit_blocks()
        return nc
    

    def s_loads(s):
        s.dma_start(out=ident_sb[:, :], in_=ident_in).then_inc(sq_h, 16)
        s.dma_start(out=b1_sb[:, :], in_=b1_in).then_inc(sq_h, 16)
        s.dma_start(out=b2h_sb[:, :], in_=b2_in).then_inc(sq_h, 16)
        s.dma_start(out=outb_sb[:, :], in_=outb_in).then_inc(sq_h, 16)
        s.dma_start(out=meta_sb[:, :], in_=meta_in).then_inc(sq_h, 16)
        s.dma_start(out=mask_sb[:, :, :], in_=mask_in).then_inc(sq_h, 16)
    em("sync", s_loads)
    c_consts = ctr["sq"].inc(6 * 16)

    def v_consts(v):
        v.wait_ge(sq_h, c_consts)
        v.memset(ones_sb[:, :], 1.0)
        v.memset(onesf_sb[:, :], 1.0)
        v.memset(onesrow_sb[:, :], 1.0)
        v.memset(eps_sb[:, :], EPS)
        v.tensor_scalar_mul(b2h_sb[:, :], b2h_sb[:, :], 0.5).then_inc(vp_h, 1)
    em("vector", v_consts)
    c_vconst = ctr["vp"].inc(1)

    def g_meta(g):
        g.wait_ge(sq_h, c_consts)
        g.reg_load(r_role_g, meta_sb[:1, 0:1])
        g.reg_load(r_q_g, meta_sb[:1, 3:4])
    em("gpsimd", g_meta)

    def s_meta(s):
        s.wait_ge(sq_h, c_consts)
        s.reg_load(r_role_s, meta_sb[:1, 0:1])
    em("sync", s_meta)

    if stage == "p1b_only":
        def s_seed(s):
            sh_r = sh_x1.rearrange("(n p) c -> p n c", p=128)
            xr = x_prod.rearrange("(n p) c -> p n c", p=128)
            with s.If_eq(r_role_s, 0):
                s.dma_start(out=sh_r[:, 0:HALF // 128, :], in_=xr).then_inc(sq_h, 16)
            with s.Else():
                s.dma_start(out=sh_r[:, HALF // 128:T // 128, :], in_=xr).then_inc(sq_h, 16)
        em("sync", s_seed)
        c_p1a_done = ctr["sq"].inc(16)
    else:
        w1_ctx = nc.sbuf_tensor("w1t_sb", [128, 8, FF], BF16)
        w2_ctx = nc.sbuf_tensor("w2t_sb", [128, 32, C], BF16)
        w1t_sb = w1_ctx.__enter__()
        w2t_sb = w2_ctx.__enter__()

    def ternarize(win_ap, n_rows, n_free, dst_view, dst_st, scale_col, extra_scale,
                  scratch_w, scratch_t, part_sb, par2_sb, ps_small):
        nt = n_rows // 128
        total = float(n_rows * n_free)
        for i in range(nt):
            def s_load(s, i=i, gate=(i == 0)):
                if gate:
                    s.wait_ge(vp_h, c_vconst)
                s.dma_start(out=scratch_w[:, 0:n_free],
                            in_=win_ap[i * 128:(i + 1) * 128, :]).then_inc(sq_h, 16)
            em("sync", s_load)
            c_l = ctr["sq"].inc(16)

            def v_abs(v, i=i, c_l=c_l):
                v.wait_ge(sq_h, c_l)
                v.reduce_sum(part_sb[:, i:i + 1], scratch_w[:, 0:n_free], AX.X,
                             apply_absolute_value=True).then_inc(vp_h, 1)
            em("vector", v_abs)
            c_abs = ctr["vp"].inc(1)

            def s_wait(s, c_abs=c_abs):
                s.wait_ge(vp_h, c_abs)
            em("sync", s_wait)

        def v_part(v):
            v.reduce_sum(par2_sb[:, 0:1], part_sb[:, 0:nt], AX.X).then_inc(vp_h, 1)
        em("vector", v_part)
        c_p = ctr["vp"].inc(1)

        def t_psum(t, c_p=c_p):
            t.wait_ge(vp_h, c_p)
            t.matmul(ps_small[0:1, 0:1], lhsT=par2_sb[:, 0:1],
                     rhs=onesf_sb[:, 0:1], start=True, stop=True).then_inc(tp_h, 1)
        em("tensor", t_psum)
        c_mm = ctr["tp"].inc(1)

        base_sc = ctr["vp"].v

        def v_scale(v, c_mm=c_mm, base=base_sc):
            v.wait_ge(tp_h, c_mm)
            v.tensor_scalar(par2_sb[:1, 1:2], ps_small[0:1, 0:1],
                            1.0 / total, 1e-8, ALU.mult, ALU.max).then_inc(vp_h, 1)
            v.wait_ge(vp_h, base + 1)
            v.reciprocal(part_sb[:1, 35:36], par2_sb[:1, 1:2]).then_inc(vp_h, 1)
        em("vector", v_scale)
        c_r = ctr["vp"].inc(2)

        def t_bcast(t, c_r=c_r):
            t.wait_ge(vp_h, c_r)
            t.matmul(ps_small[:, 1:2], lhsT=onesrow_sb[:1, :],
                     rhs=part_sb[:1, 35:36], start=True, stop=True).then_inc(tp_h, 1)
        em("tensor", t_bcast)
        c_bc = ctr["tp"].inc(1)

        def v_bc(v, c_bc=c_bc):
            v.wait_ge(tp_h, c_bc)
            v.tensor_copy(scal_sb[:, scale_col:scale_col + 1], ps_small[:, 1:2]).then_inc(vp_h, 1)
        em("vector", v_bc)
        c_sc = ctr["vp"].inc(1)

        for i in range(nt):
            def s_load2(s, i=i):
                s.dma_start(out=scratch_w[:, 0:n_free],
                            in_=win_ap[i * 128:(i + 1) * 128, :]).then_inc(sq_h, 16)
            em("sync", s_load2)
            c_l2 = ctr["sq"].inc(16)

            base_q = ctr["vp"].v

            def v_quant(v, i=i, c_l2=c_l2, c_sc=c_sc, base=base_q):
                v.wait_ge(sq_h, c_l2)
                if i == 0:
                    v.wait_ge(tp_h, c_bc)
                v.tensor_scalar(scratch_t[:, 0:n_free], scratch_w[:, 0:n_free],
                                scal_sb[:, scale_col:scale_col + 1], MAGIC,
                                ALU.mult, ALU.add).then_inc(vp_h, 1)
                v.wait_ge(vp_h, base + 1)
                v.tensor_scalar(scratch_t[:, 0:n_free], scratch_t[:, 0:n_free],
                                MAGIC + 1.0, MAGIC - 1.0, ALU.min, ALU.max).then_inc(vp_h, 1)
                v.wait_ge(vp_h, base + 2)
                v.tensor_scalar(dst_view(i), scratch_t[:, 0:n_free],
                                extra_scale, -MAGIC * extra_scale,
                                ALU.mult, ALU.add).then_inc(vp_h, 1)
            em("vector", v_quant)
            c_q = ctr["vp"].inc(3)

            def s_stash(s, i=i, c_q=c_q):
                s.wait_ge(vp_h, c_q)
                s.dma_start(out=dst_st[i * 128:(i + 1) * 128, :],
                            in_=dst_view(i)).then_inc(sq_h, 16)
            em("sync", s_stash)
            ctr["sq"].inc(16)

    if stage != "p1b_only":
        with nc.sbuf_tensor("p0_w", [128, FF], F32) as p0_w, \
             nc.sbuf_tensor("p0_t", [128, FF], BF16) as p0_t, \
             nc.sbuf_tensor("p0_part", [128, 36], F32) as p0_part, \
             nc.sbuf_tensor("p0_par2", [128, 4], F32) as p0_par2, \
             nc.psum_tensor("p0_ps", [128, 512], F32) as p0_ps:
            ternarize(w1t_in, C, FF, lambda i: w1t_sb[:, i, :], w1t_st, 0, 1.0,
                      p0_w, p0_t, p0_part, p0_par2, p0_ps)
            ternarize(w2t_in, FF, C, lambda i: w2t_sb[:, i, :], w2t_st, 1, 0.5,
                      p0_w, p0_t, p0_part, p0_par2, p0_ps)

    def emit_ln(x_view, stA, stB, stC, ln_view, c_xready, prev_tr_tp, xr_sem=None):
        base_st = ctr["vp"].v

        def v_stats(v, c_xready=c_xready, ptr=prev_tr_tp, base=base_st,
                    xs=xr_sem):
            v.wait_ge(xs if xs is not None else sq_h, c_xready)
            if ptr is not None:
                v.wait_ge(tp_h, ptr)
            v.bn_stats(stA[:, 0:6], x_view[:, 0:512])
            v.bn_stats(stA[:, 6:12], x_view[:, 512:1024]).then_inc(vp_h, 1)
            v.wait_ge(vp_h, base + 1)
            v.bn_aggr(stB[:, 0:2], stA[:, 0:12]).then_inc(vp_h, 1)
        em("vector", v_stats)
        c_var = ctr["vp"].inc(2)

        def a_sqrt(sc, c_var=c_var):
            sc.wait_ge(vp_h, c_var)
            sc.activation(stC[:, 1:2], stB[:, 1:2], AF.Sqrt, bias=eps_sb[:, :]).then_inc(ap_h, 1)
        em("scalar", a_sqrt)
        c_std = ctr["ap"].inc(1)

        base_fin = ctr["vp"].v

        def v_fin(v, c_std=c_std, base=base_fin):
            v.wait_ge(ap_h, c_std)
            v.reciprocal(stB[:, 2:3], stC[:, 1:2]).then_inc(vp_h, 1)
            v.wait_ge(vp_h, base + 1)
            v.scalar_tensor_tensor(stC[:, 2:3], stB[:, 0:1], -1.0, stB[:, 2:3],
                                   ALU.mult, ALU.mult).then_inc(vp_h, 1)
            v.wait_ge(vp_h, base + 2)
            v.tensor_scalar(ln_view, x_view, stB[:, 2:3], stC[:, 2:3],
                            ALU.mult, ALU.add).then_inc(vp_h, 1)
        em("vector", v_fin)
        c_ln = ctr["vp"].inc(3)
        return c_ln, c_var

    def emit_transposes(src_fn, nchunks, ps_bank, dst_fn, c_src_v, extra_t_waits=()):
        def t_tr(t, waits=tuple(extra_t_waits), c_src_v=c_src_v):
            for sem, val in waits:
                t.wait_ge(sem, val)
            t.wait_ge(vp_h, c_src_v)
            for cc in range(nchunks):
                mm = t.transpose(ps_bank[:, cc * 128:(cc + 1) * 128],
                                 src_fn(cc), ident_sb[:, :])
            mm.then_inc(tp_h, 1)
        em("tensor", t_tr)
        c_tr = ctr["tp"].inc(1)

        def a_evac(sc, c_tr=c_tr):
            sc.wait_ge(tp_h, c_tr)
            for cc in range(nchunks):
                ins = sc.copy(dst_fn(cc), ps_bank[:, cc * 128:(cc + 1) * 128])
            ins.then_inc(ap_h, 1)
        em("scalar", a_evac)
        return ctr["ap"].inc(1), c_tr

    def ffn_phase(x_src, nst, out_shared, w1s, w2s, st_tok=256,
                  first_load_vp=None, first_tr_sq=None, first_load_sq=None,
                  tag=""):
        ntile = st_tok // 128
        with nc.sbuf_tensor("f_x" + tag, [128, 2, ntile, C], F32) as x_sb, \
             nc.sbuf_tensor("f_stA" + tag, [128, 2, 12], F32) as stA, \
             nc.sbuf_tensor("f_stB" + tag, [128, 2, 4], F32) as stB, \
             nc.sbuf_tensor("f_stC" + tag, [128, 2, 4], F32) as stC, \
             nc.sbuf_tensor("f_ln" + tag, [128, 2, C], BF16) as ln_sb, \
             nc.sbuf_tensor("f_lnT" + tag, [128, 2, 8, st_tok], BF16) as lnT_sb, \
             nc.sbuf_tensor("f_h" + tag, [128, 32, st_tok], BF16) as h_sb, \
             nc.sbuf_tensor("f_ffn" + tag, [128, 8, st_tok], BF16) as ffn_sb, \
             nc.psum_tensor("f_ptr01" + tag, [128, 2048], BF16) as ps_tr01, \
             nc.psum_tensor("f_ph01" + tag, [128, 1024], F32) as ps_h01, \
             nc.psum_tensor("f_pd4" + tag, [128, 2048], F32) as ps_d4:
            ps_h = [ps_h01[:, 0:st_tok], ps_h01[:, 512:512 + st_tok]]
            ps_d = [ps_d4[:, i * 512:i * 512 + st_tok] for i in range(4)]
            stq = [None, None]
            prev_evB = None
            prev_h_free = None
            tr_marks = []
            res_cs = []
            ln_done = {}
            w1_user = {}

            def emit_load(st_i):
                base = st_i * st_tok

                def s_x(s, base=base, st_i=st_i, pv=stq[st_i % 2],
                        flw=first_load_vp if st_i == 0 else None,
                        fls=first_load_sq if st_i == 0 else None):
                    if pv is not None:
                        s.wait_ge(sq_h, pv)
                    if flw is not None:
                        s.wait_ge(vp_h, flw)
                    if fls is not None:
                        s.wait_ge(sq_h, fls)
                    s.dma_start(out=x_sb[:, st_i % 2, :, :],
                                in_=x_src.rearrange("(n p) c -> p n c", p=128)
                                [:, base // 128:base // 128 + ntile, :]
                                ).then_inc(sq_h, 16)
                em("sync", s_x)
                return ctr["sq"].inc(16)

            def emit_ln_dve(st_i, c_x):
                sqm = []
                for tt in range(ntile):
                    base_st2 = ctr["vp"].v
                    ptr = tr_marks[-2] if len(tr_marks) >= 2 else None

                    def v_st(v, tt=tt, st_i=st_i, c_x=c_x, base=base_st2, ptr=ptr):
                        v.wait_ge(sq_h, c_x)
                        if ptr is not None:
                            v.wait_ge(tp_h, ptr)
                        v.bn_stats(stA[:, tt, 0:6], x_sb[:, st_i % 2, tt, 0:512])
                        v.bn_stats(stA[:, tt, 6:12],
                                   x_sb[:, st_i % 2, tt, 512:1024]).then_inc(vp_h, 1)
                        v.wait_ge(vp_h, base + 1)
                        v.bn_aggr(stB[:, tt, 0:2], stA[:, tt, 0:12]).then_inc(vp_h, 1)
                    em("vector", v_st)
                    sqm.append(ctr["vp"].inc(2))
                for tt in range(ntile):
                    def a_sq(sc, tt=tt, cv=sqm[tt]):
                        sc.wait_ge(vp_h, cv)
                        sc.activation(stC[:, tt, 1:2], stB[:, tt, 1:2], AF.Sqrt,
                                      bias=eps_sb[:, :]).then_inc(ap_h, 1)
                    em("scalar", a_sq)
                    sqm[tt] = ctr["ap"].inc(1)
                c_ln = None
                for tt in range(ntile):
                    base_f = ctr["vp"].v

                    def v_f(v, tt=tt, st_i=st_i, cs=sqm[tt], base=base_f):
                        v.wait_ge(ap_h, cs)
                        v.reciprocal(stB[:, tt, 2:3], stC[:, tt, 1:2]).then_inc(vp_h, 1)
                        v.wait_ge(vp_h, base + 1)
                        v.scalar_tensor_tensor(stC[:, tt, 2:3], stB[:, tt, 0:1], -1.0,
                                               stB[:, tt, 2:3], ALU.mult,
                                               ALU.mult).then_inc(vp_h, 1)
                        v.wait_ge(vp_h, base + 2)
                        v.tensor_scalar(ln_sb[:, tt, :], x_sb[:, st_i % 2, tt, :],
                                        stB[:, tt, 2:3], stC[:, tt, 2:3],
                                        ALU.mult, ALU.add).then_inc(vp_h, 1)
                    em("vector", v_f)
                    c_ln = ctr["vp"].inc(3)
                ln_done[st_i] = c_ln

            def emit_ln_tr(st_i, extra0=()):
                for tt in range(ntile):
                    extra = list(extra0) if tt == 0 else []
                    if tt == 0 and st_i >= 2 and (st_i - 2) in w1_user:
                        extra.append((tp_h, w1_user[st_i - 2]))
                    cev, c_tr = emit_transposes(
                        lambda cc, tt=tt: ln_sb[:, tt, cc * 128:(cc + 1) * 128],
                        8, ps_tr01[:, (tt % 2) * 1024:(tt % 2) * 1024 + 1024],
                        lambda cc, tt=tt, st_i=st_i:
                            lnT_sb[:, st_i % 2, cc, tt * 128:(tt + 1) * 128],
                        ln_done[st_i], extra)
                    tr_marks.append(c_tr)
                return cev

            c_x0 = emit_load(0)
            emit_ln_dve(0, c_x0)
            extra0 = []
            if first_tr_sq is not None:
                extra0.append((sw_h, first_tr_sq))
            lnT_ready = emit_ln_tr(0, extra0)
            c_x_next = emit_load(1) if nst > 1 else None

            for st_i in range(nst):
                base = st_i * st_tok
                gelu_c = []
                for ff in range(33):
                    if ff < 32:
                        f = ff

                        def t_w1(t, f=f, st_i=st_i,
                                 lnr=lnT_ready if f == 0 else None,
                                 pevB=prev_evB if f == 0 else None,
                                 gw=gelu_c[-2] if len(gelu_c) >= 2 else None):
                            if lnr is not None:
                                t.wait_ge(ap_h, lnr)
                            if pevB is not None:
                                t.wait_ge(vp_h, pevB)
                            if gw is not None:
                                t.wait_ge(ap_h, gw)
                            for cc in range(8):
                                mm = t.matmul(ps_h[f % 2],
                                              lhsT=w1s[:, cc, f * 128:(f + 1) * 128],
                                              rhs=lnT_sb[:, st_i % 2, cc, :],
                                              start=(cc == 0), stop=(cc == 7))
                            mm.then_inc(tp_h, 1)
                        em("tensor", t_w1)
                        c_w1 = ctr["tp"].inc(1)

                        def a_gelu(sc, f=f, c_w1=c_w1):
                            sc.wait_ge(tp_h, c_w1)
                            sc.activation(h_sb[:, f, :], ps_h[f % 2], AF.Gelu,
                                          bias=b1_sb[:, f:f + 1]).then_inc(ap_h, 1)
                        em("scalar", a_gelu)
                        gelu_c.append(ctr["ap"].inc(1))
                    if ff >= 1:
                        f = ff - 1

                        def t_w2a(t, f=f, cg=gelu_c[f]):
                            t.wait_ge(ap_h, cg)
                            for d in range(4):
                                mm = t.matmul(ps_d[d],
                                              lhsT=w2s[:, f, d * 128:(d + 1) * 128],
                                              rhs=h_sb[:, f, :],
                                              start=(f == 0), stop=(f == 31))
                            mm.then_inc(tp_h, 1)
                        em("tensor", t_w2a)
                        c_w2a = ctr["tp"].inc(1)
                w1_user[st_i] = c_w1

                if st_i + 1 < nst:
                    emit_ln_dve(st_i + 1, c_x_next)

                def v_evA(v, c_w2a=c_w2a):
                    v.wait_ge(tp_h, c_w2a)
                    for d in range(4):
                        ins = v.tensor_scalar(ffn_sb[:, d, :], ps_d[d],
                                              b2h_sb[:, d:d + 1], None, ALU.add)
                    ins.then_inc(vp_h, 1)
                em("vector", v_evA)
                c_evA = ctr["vp"].inc(1)

                for f in range(32):
                    def t_w2b(t, f=f, c_evA=c_evA):
                        if f == 0:
                            t.wait_ge(vp_h, c_evA)
                        for d in range(4):
                            mm = t.matmul(ps_d[d],
                                          lhsT=w2s[:, f, (d + 4) * 128:(d + 5) * 128],
                                          rhs=h_sb[:, f, :],
                                          start=(f == 0), stop=(f == 31))
                        mm.then_inc(tp_h, 1)
                    em("tensor", t_w2b)
                    c_w2b = ctr["tp"].inc(1)
                prev_h_free = c_w2b

                def v_evB(v, c_w2b=c_w2b):
                    v.wait_ge(tp_h, c_w2b)
                    for d in range(4):
                        ins = v.tensor_scalar(ffn_sb[:, d + 4, :], ps_d[d],
                                              b2h_sb[:, d + 4:d + 5], None, ALU.add)
                    ins.then_inc(vp_h, 1)
                em("vector", v_evB)
                c_evB = ctr["vp"].inc(1)
                prev_evB = c_evB

                store_cs = []
                for tt in range(ntile):
                    def t_ftr(t, tt=tt, c_evB=c_evB,
                              pr=res_cs[-2] if len(res_cs) >= 2 else None):
                        if tt == 0:
                            t.wait_ge(vp_h, c_evB)
                        if pr is not None:
                            t.wait_ge(vp_h, pr)
                        for dc in range(8):
                            mm = t.transpose(
                                ps_tr01[:, (tt % 2) * 1024 + dc * 128:(tt % 2) * 1024 + (dc + 1) * 128],
                                ffn_sb[:, dc, tt * 128:(tt + 1) * 128],
                                ident_sb[:, :])
                        mm.then_inc(tp_h, 1)
                    em("tensor", t_ftr)
                    c_ftr = ctr["tp"].inc(1)

                    def v_res(v, tt=tt, st_i=st_i, c_ftr=c_ftr):
                        v.wait_ge(tp_h, c_ftr)
                        v.tensor_tensor(x_sb[:, st_i % 2, tt, :],
                                        x_sb[:, st_i % 2, tt, :],
                                        ps_tr01[:, (tt % 2) * 1024:(tt % 2) * 1024 + 1024],
                                        ALU.add).then_inc(vp_h, 1)
                    em("vector", v_res)
                    c_res = ctr["vp"].inc(1)
                    res_cs.append(c_res)

                    if out_shared:
                        def s_store(s, tt=tt, st_i=st_i, base=base, c_res=c_res):
                            s.wait_ge(vp_h, c_res)
                            with s.If_eq(r_role_s, 0):
                                s.dma_start(out=sh_x1[base + tt * 128: base + (tt + 1) * 128, :],
                                            in_=x_sb[:, st_i % 2, tt, :]).then_inc(sq_h, 16)
                            with s.Else():
                                s.dma_start(out=sh_x1[HALF + base + tt * 128: HALF + base + (tt + 1) * 128, :],
                                            in_=x_sb[:, st_i % 2, tt, :]).then_inc(sq_h, 16)
                    else:
                        def s_store(s, tt=tt, st_i=st_i, base=base, c_res=c_res):
                            s.wait_ge(vp_h, c_res)
                            s.dma_start(out=y_out[base + tt * 128: base + (tt + 1) * 128, :],
                                        in_=x_sb[:, st_i % 2, tt, :]).then_inc(sq_h, 16)
                    em("sync", s_store)
                    store_cs.append(ctr["sq"].inc(16))
                stq[st_i % 2] = store_cs[-1]

                if st_i + 1 < nst:
                    lnT_ready = emit_ln_tr(
                        st_i + 1,
                        [(tp_h, prev_h_free), (vp_h, res_cs[-1])])
                    c_x_next = emit_load(st_i + 2) if st_i + 2 < nst else None
            return stq[(nst - 1) % 2]

    if stage != "p1b_only":
        c_tern_done = ctr["vp"].v
        c_p1a_done = ffn_phase(x_prod, HALF // 256, True, w1t_sb, w2t_sb,
                               first_load_vp=c_tern_done)

        w2_ctx.__exit__(None, None, None)
        w1_ctx.__exit__(None, None, None)


    if stage == "p1a":
        return _finalize()
    if stage == "p1b_only":
        pass

    with nc.sbuf_tensor("q_w", [128, 8, 3 * C], BF16) as qkv_sb, \
         nc.sbuf_tensor("q_x", [128, 2, 4, C], F32) as x1_sb, \
         nc.sbuf_tensor("q_stA", [128, 12], F32) as stA, \
         nc.sbuf_tensor("q_stB", [128, 4], F32) as stB, \
         nc.sbuf_tensor("q_stC", [128, 4], F32) as stC, \
         nc.sbuf_tensor("q_ln", [128, 2, C], BF16) as ln_sb, \
         nc.sbuf_tensor("q_lnT", [128, 8, ST], BF16) as lnT_sb, \
         nc.sbuf_tensor("q_qst", [128, 2, 8, ST], BF16) as qst_sb, \
         nc.sbuf_tensor("q_kst", [128, 2, 8, ST], BF16) as kst_sb, \
         nc.sbuf_tensor("q_vst", [128, 2, 4, C], BF16) as vst_sb, \
         nc.psum_tensor("q_ptr01", [128, 2048], BF16) as ps_tr01, \
         nc.psum_tensor("q_pq01", [128, 1024], F32) as ps_q01, \
         nc.psum_tensor("q_pv4", [128, 2048], F32) as ps_v4:
        ps_tr = [ps_tr01[:, 0:1024], ps_tr01[:, 1024:2048]]
        ps_qk = [ps_q01[:, 0:512], ps_q01[:, 512:1024]]
        ps_v = [[ps_v4[:, 0:512], ps_v4[:, 512:1024]],
                [ps_v4[:, 1024:1536], ps_v4[:, 1536:2048]]]

        def g_qkvw(g, cq=c_p1a_done):
            g.wait_ge(sq_h, cq)
            g.dma_start(out=qkv_sb[:, :, :],
                        in_=qkvt_in.rearrange("(cc p) f -> p cc f", p=128)
                        ).then_inc(sw_h, 16)
        em("gpsimd", g_qkvw)
        c_qkvw = ctr["sw"].inc(16)

        prev_store = c_p1a_done
        prev_evac_a = None
        tr_marks = []
        stats_marks = []
        qk_marks = []
        vev_marks = []
        ln_marks = []
        store_marks = [[], []]
        def p1b_load(st_i):
            base = st_i * ST

            def s_x1(s, base=base, st_i=st_i,
                     pln=ln_marks[-5] if len(ln_marks) >= 5 else None,
                     pv0=c_p1a_done if st_i == 0 else None):
                if pln is not None:
                    s.wait_ge(vp_h, pln)
                if pv0 is not None:
                    s.wait_ge(sq_h, pv0)
                sh_r = sh_x1.rearrange("(n p) c -> p n c", p=128)
                with s.If_eq(r_role_s, 0):
                    s.dma_start(out=x1_sb[:, st_i % 2, :, :],
                                in_=sh_r[:, base // 128:base // 128 + 4, :]
                                ).then_inc(sx1_h[st_i % 2], 16)
                with s.Else():
                    s.dma_start(out=x1_sb[:, st_i % 2, :, :],
                                in_=sh_r[:, (HALF + base) // 128:(HALF + base) // 128 + 4, :]
                                ).then_inc(sx1_h[st_i % 2], 16)
            em("sync", s_x1)
            return ctr["sx1%d" % (st_i % 2)].inc(16)

        c_x_cur = p1b_load(0)
        for st_i in range(NST1):
            base = st_i * ST
            c_x = c_x_cur
            c_xt = [c_x] * 4
            lnT_ready = None
            for tt in range(4):
                ptr = tr_marks[-2] if len(tr_marks) >= 2 else None
                psv = stats_marks[-1] if stats_marks else None
                c_ln, c_var = emit_ln(x1_sb[:, st_i % 2, tt, :], stA, stB, stC,
                                      ln_sb[:, tt % 2, :], c_xt[tt], ptr,
                                      xr_sem=sx1_h[st_i % 2])
                stats_marks.append(c_var)
                ln_marks.append(c_ln)
                cev, c_tr = emit_transposes(
                    lambda cc, tt=tt: ln_sb[:, tt % 2, cc * 128:(cc + 1) * 128],
                    8, ps_tr[tt % 2],
                    lambda cc, tt=tt: lnT_sb[:, cc, tt * 128:(tt + 1) * 128],
                    c_ln)
                tr_marks.append(c_tr)
                lnT_ready = cev
            for which in ("q", "k"):
                colbase = 0 if which == "q" else C
                scale = (1.0 / 32.0) if which == "q" else 1.0
                dstst = qst_sb if which == "q" else kst_sb
                for fc in range(8):
                    def t_p(t, fc=fc, colbase=colbase, which=which,
                            lnr=lnT_ready if (fc == 0 and which == "q") else None,
                            pea=qk_marks[-2] if len(qk_marks) >= 2 else None):
                        if lnr is not None:
                            t.wait_ge(ap_h, lnr)
                            t.wait_ge(sw_h, c_qkvw)
                        if pea is not None:
                            t.wait_ge(ap_h, pea)
                        for cc in range(8):
                            mm = t.matmul(ps_qk[fc % 2],
                                          lhsT=qkv_sb[:, cc, colbase + fc * 128: colbase + (fc + 1) * 128],
                                          rhs=lnT_sb[:, cc, :],
                                          start=(cc == 0), stop=(cc == 7))
                        mm.then_inc(tp_h, 1)
                    em("tensor", t_p)
                    c_mm = ctr["tp"].inc(1)

                    def a_ev(sc, fc=fc, dstst=dstst, scale=scale, c_mm=c_mm,
                             which=which, st_i=st_i,
                             pstore=(store_marks[st_i % 2][-1] if store_marks[st_i % 2] else None)
                             if (fc == 0 and which == "q") else None):
                        sc.wait_ge(tp_h, c_mm)
                        if pstore is not None:
                            sc.wait_ge(qst_h[st_i % 2], pstore)
                        sc.activation(dstst[:, st_i % 2, fc, :], ps_qk[fc % 2],
                                      AF.Copy, scale=scale).then_inc(ap_h, 1)
                    em("scalar", a_ev)
                    prev_evac_a = ctr["ap"].inc(1)
                    qk_marks.append(prev_evac_a)
            for tt in range(4):
                def t_v(t, tt=tt,
                        pea=prev_evac_a if tt == 0 else None,
                        pvv=vev_marks[-2] if len(vev_marks) >= 2 else None):
                    if pea is not None:
                        t.wait_ge(ap_h, pea)
                    if pvv is not None:
                        t.wait_ge(ap_h, pvv)
                    for half in range(2):
                        for cc in range(8):
                            mm = t.matmul(ps_v[tt % 2][half],
                                          lhsT=lnT_sb[:, cc, tt * 128:(tt + 1) * 128],
                                          rhs=qkv_sb[:, cc, 2 * C + half * 512: 2 * C + (half + 1) * 512],
                                          start=(cc == 0), stop=(cc == 7))
                    mm.then_inc(tp_h, 1)
                em("tensor", t_v)
                c_vm = ctr["tp"].inc(1)

                def a_vev(sc, tt=tt, c_vm=c_vm, st_i=st_i):
                    sc.wait_ge(tp_h, c_vm)
                    for half in range(2):
                        ins = sc.copy(vst_sb[:, st_i % 2, tt, half * 512:(half + 1) * 512],
                                      ps_v[tt % 2][half])
                    ins.then_inc(ap_h, 1)
                em("scalar", a_vev)
                prev_evac_a = ctr["ap"].inc(1)
                vev_marks.append(prev_evac_a)
            c_allev = prev_evac_a

            def s_qkvout(s, base=base, c_allev=c_allev, st_i=st_i):
                s.wait_ge(ap_h, c_allev)
                with s.If_eq(r_role_s, 0):
                    s.dma_start(out=sh_qT.rearrange("(cc p) t -> p cc t", p=128)[:, :, base:base + ST],
                                in_=qst_sb[:, st_i % 2, :, :]).then_inc(qst_h[st_i % 2], 16)
                    s.dma_start(out=sh_kT.rearrange("(cc p) t -> p cc t", p=128)[:, :, base:base + ST],
                                in_=kst_sb[:, st_i % 2, :, :]).then_inc(qst_h[st_i % 2], 16)
                    s.dma_start(out=sh_v.rearrange("(n p) c -> p n c", p=128)[:, base // 128:base // 128 + 4, :],
                                in_=vst_sb[:, st_i % 2, :, :]).then_inc(qst_h[st_i % 2], 16)
                with s.Else():
                    s.dma_start(out=sh_qT.rearrange("(cc p) t -> p cc t", p=128)[:, :, HALF + base:HALF + base + ST],
                                in_=qst_sb[:, st_i % 2, :, :]).then_inc(qst_h[st_i % 2], 16)
                    s.dma_start(out=sh_kT.rearrange("(cc p) t -> p cc t", p=128)[:, :, HALF + base:HALF + base + ST],
                                in_=kst_sb[:, st_i % 2, :, :]).then_inc(qst_h[st_i % 2], 16)
                    s.dma_start(out=sh_v.rearrange("(n p) c -> p n c", p=128)[:, (HALF + base) // 128:(HALF + base) // 128 + 4, :],
                                in_=vst_sb[:, st_i % 2, :, :]).then_inc(qst_h[st_i % 2], 16)
            if st_i + 1 < NST1:
                c_x_cur = p1b_load(st_i + 1)
            em("sync", s_qkvout)
            prev_store = ctr["qst%d" % (st_i % 2)].inc(48)
            store_marks[st_i % 2].append(prev_store)
        c_p1b_q0 = ctr["qst0"].v
        c_p1b_q1 = ctr["qst1"].v

    if stage in ("p1b", "p1b_only"):
        return _finalize()

    c_sq_prebar = ctr["sq"].v

    def g_barrier(g, q0=c_p1b_q0, q1=c_p1b_q1, c_sq_prebar=c_sq_prebar):
        g.wait_ge(qst_h[0], q0)
        g.wait_ge(qst_h[1], q1)
        g.wait_ge(sq_h, c_sq_prebar)
        g.reg_load(r_exp_g, meta_sb[:1, 1:2])
        g.reg_save(flag_sb[:1, :1], r_exp_g)
        g.reg_mov(r_tmp_g, 0)
        g.reg_add(r_tmp_g, r_role_g, 0)
        g.reg_mul(r_tmp_g, r_tmp_g, 16)
        g.dma_start(bass.AP(sh_flag.tensor, r_tmp_g, [[1, 1], [1, 1]]),
                    flag_sb[:1, :1]).then_inc(pq_h, 16)
        g.reg_mov(r_sc_g, 16)
        g.wait_ge(pq_h, r_sc_g)
        g.reg_load(r_exp_g, meta_sb[:1, 2:3])
        g.reg_mov(r_tmp_g, 1)
        g.reg_sub(r_tmp_g, r_tmp_g, r_role_g)
        g.reg_mul(r_tmp_g, r_tmp_g, 16)
        g.reg_mov(r_diff_g, 1)
        with g.While(r_diff_g):
            g.dma_start(flag_sb[:1, :1],
                        bass.AP(sh_flag.tensor, r_tmp_g, [[1, 1], [1, 1]])).then_inc(pq_h, 16)
            g.reg_add(r_sc_g, r_sc_g, 16)
            g.wait_ge(pq_h, r_sc_g)
            g.reg_load(r_val_g, flag_sb[:1, :1])
            g.reg_sub(r_diff_g, r_val_g, r_exp_g)
        g.engine_nop().then_inc(bar_h, 1)
    em("gpsimd", g_barrier)
    c_bar = ctr["bar"].inc(1)

    if stage == "bar":
        def g_barfin(g):
            g.wait_ge(bar_h, c_bar)
        em("gpsimd", g_barfin)
        return _finalize()

    with nc.sbuf_tensor("a_ow", [128, 8, C], BF16) as outw_sb, \
         nc.sbuf_tensor("a_k", [128, 2, 8, 1024], BF16) as k_sb, \
         nc.sbuf_tensor("a_v", [128, 2, 8, 1024], BF16) as v_sb, \
         nc.sbuf_tensor("a_q", [128, 2, 8, 256], BF16) as q_sb, \
         nc.sbuf_tensor("a_x1", [128, 2, 2, C], F32) as x1s_sb, \
         nc.sbuf_tensor("a_pt", [128, 8, 256], BF16) as pt_sb, \
         nc.sbuf_tensor("a_o", [128, 2, C], F32) as o_sb, \
         nc.sbuf_tensor("a_l", [128, 8], F32) as l_sb, \
         nc.sbuf_tensor("a_linv", [128, 2], F32) as linv_sb, \
         nc.sbuf_tensor("a_at", [128, 2, C], BF16) as attn_sb, \
         nc.sbuf_tensor("a_att", [128, 8, 256], BF16) as attnT_sb, \
         nc.sbuf_tensor("a_y", [128, 8, 256], BF16) as y_sb, \
         nc.sbuf_tensor("a_x2", [128, 2, C], F32) as x2_sb, \
         nc.psum_tensor("a_ps01", [128, 1024], F32) as ps_s01, \
         nc.psum_tensor("a_po01", [128, 2048], F32) as ps_o01, \
         nc.psum_tensor("a_pl", [128, 512], F32) as ps_l, \
         nc.psum_tensor("a_ptb", [128, 1024], BF16) as ps_tb:
        ps_s = [ps_s01[:, 0:512], ps_s01[:, 512:1024]]
        ps_o = [ps_o01[:, 0:1024], ps_o01[:, 1024:2048]]

        def g_outw(g, q0=c_p1b_q0, q1=c_p1b_q1):
            g.wait_ge(qst_h[0], q0)
            g.wait_ge(qst_h[1], q1)
            g.dma_start(out=outw_sb[:, :, :],
                        in_=outt_in.rearrange("(cc p) d -> p cc d", p=128)
                        ).then_inc(sw_h, 16)
        em("gpsimd", g_outw)
        c_outw = ctr["sw"].inc(16)

        exp_marks = []
        kv_user = {0: None, 1: None}
        q_user = {0: None, 1: None}
        x1_user = {0: None, 1: None}
        oflush = {0: None, 1: None}
        lflush = None
        last_x2_store = None
        prev_psS_user = None
        c_av_last = None
        for k in range(NSLOT):
            nsb = k + 1
            kbuf = k % 2

            def g_qload(g, k=k, kbuf=kbuf, first=(k == 0),
                        qu=q_user[kbuf], xu=x1_user[kbuf]):
                if first:
                    g.wait_ge(bar_h, c_bar)
                if qu is not None:
                    g.wait_ge(tp_h, qu)
                if xu is not None:
                    g.wait_ge(vp_h, xu)

                def _loads(qv):
                    s0 = (4 * k + qv) * 256
                    g.dma_start(out=q_sb[:, kbuf, :, :],
                                in_=sh_qT.rearrange("(cc p) t -> p cc t", p=128)
                                [:, :, s0:s0 + 256]).then_inc(gqq_h[kbuf], 16)
                    g.dma_start(out=x1s_sb[:, kbuf, :, :],
                                in_=sh_x1.rearrange("(n p) c -> p n c", p=128)
                                [:, s0 // 128:s0 // 128 + 2, :]).then_inc(gqq_h[kbuf], 16)
                with g.If_eq(r_q_g, 0):
                    _loads(0)
                with g.Else():
                    with g.If_eq(r_q_g, 1):
                        _loads(1)
                    with g.Else():
                        with g.If_eq(r_q_g, 2):
                            _loads(2)
                        with g.Else():
                            _loads(3)
            em("gpsimd", g_qload)
            c_qld = ctr["gqq%d" % kbuf].inc(32)

            flushes = []
            for j in range(nsb):
                jbuf = j % 2

                def s_kv(s, j=j, jbuf=jbuf, first=(k == 0 and j == 0),
                         ku=kv_user[jbuf]):
                    if first:
                        s.wait_ge(bar_h, c_bar)
                    if ku is not None:
                        s.wait_ge(tp_h, ku)
                    s.dma_start(out=k_sb[:, jbuf, :, :],
                                in_=sh_kT.rearrange("(cc p) t -> p cc t", p=128)
                                [:, :, j * 1024:(j + 1) * 1024]).then_inc(skv_h[jbuf], 16)
                    s.dma_start(out=v_sb[:, jbuf, :, :],
                                in_=sh_v.rearrange("(n p) c -> p n c", p=128)
                                [:, j * 8:(j + 1) * 8, :]).then_inc(skv_h[jbuf], 16)
                em("sync", s_kv)
                c_kv = ctr["skv%d" % jbuf].inc(32)

                for kt in range(8):
                    def t_qk(t, kt=kt, jbuf=jbuf, c_kv=c_kv, c_qld=c_qld,
                             kbuf=kbuf,
                             pw=exp_marks[-2] if len(exp_marks) >= 2 else None,
                             psu=prev_psS_user if (j == 0 and kt == 0) else None):
                        if kt == 0:
                            t.wait_ge(skv_h[jbuf], c_kv)
                            t.wait_ge(gqq_h[kbuf], c_qld)
                            if psu is not None:
                                t.wait_ge(vp_h, psu)
                        if pw is not None:
                            t.wait_ge(ap_h, pw)
                        for cc in range(8):
                            mm = t.matmul(ps_s[kt % 2][:, 0:256],
                                          lhsT=k_sb[:, jbuf, cc, kt * 128:(kt + 1) * 128],
                                          rhs=q_sb[:, kbuf, cc, :],
                                          start=(cc == 0), stop=(cc == 7))
                        mm.then_inc(tp_h, 1)
                    em("tensor", t_qk)
                    c_qk = ctr["tp"].inc(1)

                    def a_exp(sc, kt=kt, c_qk=c_qk):
                        sc.wait_ge(tp_h, c_qk)
                        sc.activation(pt_sb[:, kt, :], ps_s[kt % 2][:, 0:256],
                                      AF.Exp).then_inc(ap_h, 1)
                    em("scalar", a_exp)
                    exp_marks.append(ctr["ap"].inc(1))

                c_mask = None
                if j == nsb - 1:
                    def v_mask(v, ce=exp_marks[-1]):
                        v.wait_ge(ap_h, ce)
                        for kt in range(8):
                            ins = v.tensor_tensor(pt_sb[:, kt, :], pt_sb[:, kt, :],
                                                  mask_sb[:, kt, :], ALU.mult)
                        ins.then_inc(vp_h, 1)
                    em("vector", v_mask)
                    c_mask = ctr["vp"].inc(1)

                for qs in range(2):
                    def t_av(t, qs=qs, jbuf=jbuf, j=j, ce=exp_marks[-1],
                             cm=c_mask,
                             ofl=oflush[qs], lfl=lflush if qs == 0 else None):
                        if cm is not None:
                            t.wait_ge(vp_h, cm)
                        if ofl is not None:
                            t.wait_ge(vp_h, ofl)
                        if lfl is not None:
                            t.wait_ge(vp_h, lfl)
                        for kt in range(8):
                            st = (kt == 0)
                            t.wait_ge(ap_h, ce - 7 + kt)
                            t.matmul(ps_o[qs][:, 0:512],
                                     lhsT=pt_sb[:, kt, qs * 128:(qs + 1) * 128],
                                     rhs=v_sb[:, jbuf, kt, 0:512],
                                     start=st, stop=False)
                            t.matmul(ps_o[qs][:, 512:1024],
                                     lhsT=pt_sb[:, kt, qs * 128:(qs + 1) * 128],
                                     rhs=v_sb[:, jbuf, kt, 512:1024],
                                     start=st, stop=False)
                            mm = t.matmul(ps_l[:, qs * 8:qs * 8 + 1],
                                          lhsT=pt_sb[:, kt, qs * 128:(qs + 1) * 128],
                                          rhs=ones_sb[:, :],
                                          start=st, stop=(kt == 7))
                        mm.then_inc(tp_h, 1)
                    em("tensor", t_av)
                    c_av = ctr["tp"].inc(1)
                    c_av_last = c_av
                    kv_user[jbuf] = c_av
                    q_user[kbuf] = c_av

                    def v_oflush(v, qs=qs, j=j, c_av=c_av):
                        v.wait_ge(tp_h, c_av)
                        if j == 0:
                            v.tensor_copy(o_sb[:, qs, :], ps_o[qs]).then_inc(vp_h, 1)
                        else:
                            v.tensor_tensor(o_sb[:, qs, :], o_sb[:, qs, :],
                                            ps_o[qs], ALU.add).then_inc(vp_h, 1)
                    em("vector", v_oflush)
                    oflush[qs] = ctr["vp"].inc(1)
                    flushes.append(oflush[qs])

                def v_lflush(v, j=j, c_av=c_av_last):
                    v.wait_ge(tp_h, c_av)
                    if j == 0:
                        v.tensor_copy(l_sb[:, 0:1], ps_l[:, 0:1])
                        v.tensor_copy(l_sb[:, 1:2], ps_l[:, 8:9]).then_inc(vp_h, 1)
                    else:
                        v.tensor_tensor(l_sb[:, 0:1], l_sb[:, 0:1], ps_l[:, 0:1], ALU.add)
                        v.tensor_tensor(l_sb[:, 1:2], l_sb[:, 1:2], ps_l[:, 8:9],
                                        ALU.add).then_inc(vp_h, 1)
                em("vector", v_lflush)
                lflush = ctr["vp"].inc(1)

            base_nm = ctr["vp"].v

            def v_norm(v, cf=lflush, co0=oflush[0], co1=oflush[1], base=base_nm):
                v.wait_ge(vp_h, max(cf, co0, co1))
                v.reciprocal(linv_sb[:, 0:2], l_sb[:, 0:2]).then_inc(vp_h, 1)
                v.wait_ge(vp_h, base + 1)
                for qs in range(2):
                    ins = v.tensor_scalar(attn_sb[:, qs, :], o_sb[:, qs, :],
                                          linv_sb[:, qs:qs + 1], None, ALU.mult)
                ins.then_inc(vp_h, 1)
            em("vector", v_norm)
            c_norm = ctr["vp"].inc(2)

            ev_c = None
            for qs in range(2):
                def t_atr(t, qs=qs, c_norm=c_norm, ev_c=ev_c):
                    if qs == 0:
                        t.wait_ge(vp_h, c_norm)
                    if ev_c is not None:
                        t.wait_ge(ap_h, ev_c)
                    for cc in range(8):
                        mm = t.transpose(ps_tb[:, cc * 128:(cc + 1) * 128],
                                         attn_sb[:, qs, cc * 128:(cc + 1) * 128],
                                         ident_sb[:, :])
                    mm.then_inc(tp_h, 1)
                em("tensor", t_atr)
                c_atr = ctr["tp"].inc(1)

                def a_aev(sc, qs=qs, c_atr=c_atr):
                    sc.wait_ge(tp_h, c_atr)
                    for cc in range(8):
                        ins = sc.copy(attnT_sb[:, cc, qs * 128:(qs + 1) * 128],
                                      ps_tb[:, cc * 128:(cc + 1) * 128])
                    ins.then_inc(ap_h, 1)
                em("scalar", a_aev)
                ev_c = ctr["ap"].inc(1)

            proj_v = []
            for dc in range(8):
                def t_proj(t, dc=dc, ev_c=ev_c, c_outw=c_outw,
                           pv=proj_v[-2] if len(proj_v) >= 2 else None):
                    if dc == 0:
                        t.wait_ge(ap_h, ev_c)
                        t.wait_ge(sw_h, c_outw)
                    if pv is not None:
                        t.wait_ge(vp_h, pv)
                    for cc in range(8):
                        mm = t.matmul(ps_s[dc % 2][:, 0:256],
                                      lhsT=outw_sb[:, cc, dc * 128:(dc + 1) * 128],
                                      rhs=attnT_sb[:, cc, :],
                                      start=(cc == 0), stop=(cc == 7))
                    mm.then_inc(tp_h, 1)
                em("tensor", t_proj)
                c_pj = ctr["tp"].inc(1)

                def v_pev(v, dc=dc, c_pj=c_pj):
                    v.wait_ge(tp_h, c_pj)
                    v.tensor_scalar(y_sb[:, dc, :], ps_s[dc % 2][:, 0:256],
                                    outb_sb[:, dc:dc + 1], None, ALU.add).then_inc(vp_h, 1)
                em("vector", v_pev)
                proj_v.append(ctr["vp"].inc(1))
            prev_psS_user = proj_v[-1]

            res_vs = []
            for qs in range(2):
                def t_ytr(t, qs=qs, pv=proj_v[-1],
                          pr=res_vs[-1] if res_vs else None):
                    if qs == 0:
                        t.wait_ge(vp_h, pv)
                    if pr is not None:
                        t.wait_ge(vp_h, pr)
                    for dc in range(8):
                        mm = t.transpose(ps_tb[:, dc * 128:(dc + 1) * 128],
                                         y_sb[:, dc, qs * 128:(qs + 1) * 128],
                                         ident_sb[:, :])
                    mm.then_inc(tp_h, 1)
                em("tensor", t_ytr)
                c_ytr = ctr["tp"].inc(1)

                def v_x2(v, qs=qs, k=k, kbuf=kbuf, c_ytr=c_ytr,
                         lxs=last_x2_store if qs == 0 else None):
                    v.wait_ge(tp_h, c_ytr)
                    if lxs is not None:
                        v.wait_ge(sq_h, lxs)
                    v.tensor_tensor(x2_sb[:, qs, :], x1s_sb[:, kbuf, qs, :],
                                    ps_tb[:, :], ALU.add).then_inc(vp_h, 1)
                em("vector", v_x2)
                res_vs.append(ctr["vp"].inc(1))
            x1_user[kbuf] = res_vs[-1]

            def s_x2(s, k=k, cr=res_vs[-1]):
                s.wait_ge(vp_h, cr)
                s.dma_start(out=x2_dram.rearrange("(n p) c -> p n c", p=128)
                            [:, k * 2:k * 2 + 2, :],
                            in_=x2_sb[:, :, :]).then_inc(sq_h, 16)
            em("sync", s_x2)
            last_x2_store = ctr["sq"].inc(16)

    if stage == "p2":
        return _finalize()

    w1_ctx2 = nc.sbuf_tensor("w1b_sb", [128, 8, FF], BF16)
    w2_ctx2 = nc.sbuf_tensor("w2b_sb", [128, 32, C], BF16)
    w1t_sb = w1_ctx2.__enter__()
    w2t_sb = w2_ctx2.__enter__()

    def s_wreload(s):
        s.dma_start(out=w1t_sb[:, :, :],
                    in_=w1t_st.rearrange("(cc p) f -> p cc f", p=128)).then_inc(sw_h, 16)
        s.dma_start(out=w2t_sb[:, :, :],
                    in_=w2t_st.rearrange("(fc p) d -> p fc d", p=128)).then_inc(sw_h, 16)
    em("sync", s_wreload)
    c_wrl = ctr["sw"].inc(32)

    ffn_phase(x2_dram, QROWS // 256, False, w1t_sb, w2t_sb,
              first_tr_sq=c_wrl, first_load_sq=last_x2_store, tag="b")

    w2_ctx2.__exit__(None, None, None)
    w1_ctx2.__exit__(None, None, None)

    return _finalize()



def kernel(L0_attn_x, L0_attn_qkv_w, L0_attn_out_w, L0_attn_out_b,
           L0_ffn_W1, L0_ffn_b1, L0_ffn_W2, L0_ffn_b2, _debug=False, _stage="all"):
    x = np.ascontiguousarray(np.asarray(L0_attn_x, dtype=np.float32))
    w1t = np.ascontiguousarray(np.asarray(L0_ffn_W1, np.float32).T)
    w2t = np.ascontiguousarray(np.asarray(L0_ffn_W2, np.float32).T)
    qkvt = np.ascontiguousarray(np.asarray(L0_attn_qkv_w, np.float32).T)
    outt = np.ascontiguousarray(np.asarray(L0_attn_out_w, np.float32).T)
    b1r = np.ascontiguousarray(np.asarray(L0_ffn_b1, np.float32).reshape(FF // 128, 128).T)
    b2r = np.ascontiguousarray(np.asarray(L0_ffn_b2, np.float32).reshape(C // 128, 128).T)
    outbr = np.ascontiguousarray(np.asarray(L0_attn_out_b, np.float32).reshape(C // 128, 128).T)
    ident = np.eye(128, dtype=ml_dtypes.bfloat16)

    nonces = np.random.randint(1, 2 ** 30, size=NCORES).astype(np.int32)
    in_maps = []
    for c in range(NCORES):
        b = c // 4
        q = c % 4
        role = c % 2
        xp = np.ascontiguousarray(x[b, role * HALF:(role + 1) * HALF, :])
        i_idx = np.arange(128)[:, None]
        j_idx = np.arange(256)[None, :]
        mask = np.zeros((128, 8, 256), dtype=ml_dtypes.bfloat16)
        for t in range(8):
            mask[:, t, :] = (t * 128 + i_idx <= q * 256 + j_idx).astype(ml_dtypes.bfloat16)
        meta = np.array([[role, nonces[c], nonces[c ^ 1], q, 0, 0, 0, 0]], dtype=np.int32)
        in_maps.append({
            "x_prod": xp, "w1t_in": w1t, "w2t_in": w2t, "qkvt_in": qkvt,
            "outt_in": outt, "b1_in": b1r, "b2_in": b2r, "outb_in": outbr,
            "ident_in": ident, "mask_in": mask, "meta_in": meta,
        })

    nc = build_program(debug_taps=_debug, stage=_stage)
    res = run_bass_kernel_spmd(nc, in_maps, core_ids=list(range(NCORES)))
    if _debug:
        kernel._debug_res = res

    out = np.empty((B, T, C), dtype=np.float32)
    for c in range(NCORES):
        b = c // 4
        q = c % 4
        y = res.results[c]["y_out"]
        for k in range(NSLOT):
            s = 4 * k + q
            out[b, s * 256:(s + 1) * 256, :] = y[k * 256:(k + 1) * 256, :]
    return out
